# version 63
# baseline (speedup 1.0000x reference)
import numpy as np
import ml_dtypes
import concourse.bass as bass
import concourse.mybir as mybir
from concourse.bass_utils import run_bass_kernel_spmd

F32 = mybir.dt.float32
BF16 = mybir.dt.bfloat16
AF = mybir.ActivationFunctionType
ALU = mybir.AluOpType

D = 1024
GD = 8
DSTATE = 16
DTRANK = 64
E = 8
T = 256

DBG_CHUNK = None
RECORD = False
LAST_SCHED = None

ENGINES = ("sp", "pe", "act", "dve", "gp")
ENG_SEM = {e: "c_" + e for e in ENGINES}


class Sched:
    def __init__(self):
        self.ops = []
        self.dma_sems = set()

    def add(self, engine, fn, reads=(), writes=(), dma=None, tag=""):
        if dma is not None:
            self.dma_sems.add(dma)
        self.ops.append(dict(e=engine, fn=fn, r=tuple(reads), w=tuple(writes),
                             dma=dma, tag=tag, waits=None, inc=None))

    def resolve(self):
        last_w = {}
        readers = {}
        dma_last = {}
        edges = [[] for _ in self.ops]
        for i, op in enumerate(self.ops):
            deps = set()
            for k in op["r"]:
                if k in last_w:
                    deps.add(last_w[k])
            for k in op["w"]:
                if k in last_w:
                    deps.add(last_w[k])
                for r in readers.get(k, ()):
                    deps.add(r)
            if op["dma"] is not None and op["dma"] in dma_last:
                deps.add(dma_last[op["dma"]])
            deps.discard(i)
            edges[i] = sorted(deps)
            for k in op["r"]:
                readers.setdefault(k, []).append(i)
            for k in op["w"]:
                last_w[k] = i
                readers[k] = []
            if op["dma"] is not None:
                dma_last[op["dma"]] = i

        need_inc = [False] * len(self.ops)
        for i, op in enumerate(self.ops):
            for j in edges[i]:
                p = self.ops[j]
                if p["dma"] is None and op["dma"] is None and \
                        p["e"] == "pe" and op["e"] == "pe":
                    continue
                need_inc[j] = True
        for i, op in enumerate(self.ops):
            if op["dma"] is not None:
                need_inc[i] = True

        ev = [None] * len(self.ops)
        eng_cnt = {e: 0 for e in ENGINES}
        dma_cnt = {}
        for i, op in enumerate(self.ops):
            if not need_inc[i]:
                continue
            if op["dma"] is not None:
                s = op["dma"]
                dma_cnt[s] = dma_cnt.get(s, 0) + 1
                ev[i] = (s, dma_cnt[s] * 16)
                op["inc"] = (s, 16)
            else:
                s = ENG_SEM[op["e"]]
                eng_cnt[op["e"]] += 1
                ev[i] = (s, eng_cnt[op["e"]])
                op["inc"] = (s, 1)
        clocks = {e: {} for e in ENGINES}
        for i, op in enumerate(self.ops):
            e = op["e"]
            waits = []
            for j in edges[i]:
                p = self.ops[j]
                if p["dma"] is None and op["dma"] is None and \
                        p["e"] == "pe" and op["e"] == "pe":
                    continue
                sem, val = ev[j]
                if clocks[e].get(sem, 0) >= val:
                    continue
                waits.append((sem, val))
                clocks[e][sem] = val
            op["waits"] = waits
        self.final = {("c_" + e): c for e, c in eng_cnt.items()}
        self.final.update({s: c * 16 for s, c in dma_cnt.items()})


def build_nc(L):
    NCH = L // T
    nc = bass.Bass("TRN2", target_bir_lowering=False)
    ap = {}

    def din(name, shape, dt=F32):
        ap[name] = nc.dram_tensor(name, list(shape), dt, kind="ExternalInput").ap()

    din("x_t", (D, L))
    ap["out"] = nc.dram_tensor("out", [L, D], F32, kind="ExternalOutput").ap()
    if DBG_CHUNK is not None:
        for nm, shape in [("dbg_gf", (128, 2048)), ("dbg_hg", (128, 2048)),
                          ("dbg_gel", (128, 2048)), ("dbg_ge", (128, 512)),
                          ("dbg_xz", (128, 16 * 259)), ("dbg_bb", (128, 4096)),
                          ("dbg_cb", (128, 4096)), ("dbg_h", (128, 4096)),
                          ("dbg_da", (128, 2048)), ("dbg_delta", (128, 2048)),
                          ("dbg_u", (128, 2048))]:
            ap[nm] = nc.dram_tensor(nm, list(shape), F32, kind="ExternalOutput").ap()

    din("w_in", (8, 128, 2048))
    din("w_out", (16, 128, 1024))
    din("w_w1", (8, 8, 128, 512))
    din("w_w2", (8, 4, 128, 1024))
    din("w_ws1", (8, 128, 512))
    din("w_ws2", (4, 128, 1024))
    din("w_cd", (2, 4, 128, 1024))
    din("aw_alog", (128, 128))
    din("aw_dtb", (128, 8))
    din("aw_dd", (128, 1024))
    din("aw_b1", (128, 32))
    din("aw_bs1", (128, 4))
    din("aw_bs2", (128, 8))
    din("aw_rs", (128, 1))
    din("aw_xpw", (128, 768))
    din("aw_dtw", (64, 1024))
    din("aw_gw", (128, 64))
    din("aw_b2", (8, 1024))
    din("cident", (128, 128))
    din("cidentb", (128, 128), BF16)
    din("cones1", (1, 128))
    din("conesc", (128, 1), BF16)

    blobs = {}
    blobs["bc_dram"] = nc.dram_tensor("bc_dram", [48, 256], BF16, kind="Internal").ap()
    blobs["ge_dram"] = nc.dram_tensor("ge_dram", [8, 256], BF16, kind="Internal").ap()
    for nm, shape in [("blob_in", (8, 128, 2048)), ("blob_out", (16, 128, 1024)),
                      ("blob_w1", (8, 8, 128, 512)), ("blob_w2", (8, 4, 128, 1024)),
                      ("blob_ws1", (8, 128, 512)), ("blob_ws2", (4, 128, 1024)),
                      ("blob_cd", (2, 4, 128, 1024))]:
        blobs[nm] = nc.dram_tensor(nm, list(shape), BF16, kind="Internal").ap()

    from contextlib import ExitStack
    with ExitStack() as ctx:
        En = ctx.enter_context
        sb = {}

        def sbuf(name, shape, dt):
            sb[name] = En(nc.sbuf_tensor(name, list(shape), dt))

        sbuf("x_f", (128, 2 * 8 * 258), F32)
        sbuf("xn_b", (128, 8 * 258), BF16)
        sbuf("scrb1", (128, 2080), BF16)
        sbuf("scrb2", (128, 2080), BF16)
        sbuf("lnab", (128, 516), F32)
        sbuf("strow", (1, 1032), F32)
        sbuf("xzh", (128, 16 * 259), BF16)
        sbuf("xc", (128, 1024), BF16)
        sbuf("xco", (128, 2048), BF16)
        sbuf("zc", (128, 2048), BF16)
        sbuf("dtr", (64, 256), BF16)
        sbuf("bcB", (48, 256), BF16)
        sbuf("delta", (128, 2080), F32)
        sbuf("u_b", (128, 2048), BF16)
        sbuf("Bb", (128, 4096), BF16)
        sbuf("Cb", (128, 4096), BF16)
        sbuf("xzc", (128, 48), BF16)
        sbuf("dA", (128, 2048), F32)
        sbuf("fixt", (128, 4), F32)
        sbuf("dBx", (128, 2 * 4096), BF16)
        sbuf("h_b", (128, 4096), BF16)
        sbuf("carry", (128, 128), F32)
        sbuf("y_b", (128, 2048), BF16)
        sbuf("s_f", (128, 2080), F32)
        sbuf("g_f", (128, 2048), F32)
        sbuf("hsh", (128, 1024), BF16)
        sbuf("gl_f", (8, 256), F32)
        sbuf("st_f", (128, 16), F32)
        sbuf("m8", (128, 16), F32)
        sbuf("den", (128, 2), F32)
        sbuf("rden", (128, 2), F32)
        sbuf("ind", (128, 16), F32)
        sbuf("gt", (128, 16), F32)
        sbuf("gcm", (8, 256), BF16)
        sbuf("ge_all", (128, 2048), BF16)
        sbuf("stage", (128, 1024), F32)
        sbuf("sb_in", (128, 4 * 1024), BF16)
        sbuf("sb_cd", (128, 2 * 1024), BF16)
        sbuf("sb_w1", (128, 2 * 2048), BF16)
        sbuf("sb_w2", (128, 2 * 2048), BF16)
        sbuf("sb_out", (128, 2 * 2048), BF16)
        sbuf("ws1_b", (128, 4096), BF16)
        sbuf("ws2_b", (128, 4096), BF16)
        sbuf("ident_f", (128, 128), F32)
        sbuf("ident_b", (128, 128), BF16)
        sbuf("ones1", (1, 128), F32)
        sbuf("onescol", (128, 1), BF16)
        sbuf("A_neg", (128, 128), F32)
        sbuf("dtb", (128, 8), F32)
        sbuf("b1c", (128, 32), F32)
        sbuf("bs1c", (128, 4), F32)
        sbuf("bs2c", (128, 8), F32)
        sbuf("rsh", (128, 1), F32)
        sbuf("rs_c", (128, 1), F32)
        sbuf("xpw", (128, 768), BF16)
        sbuf("dtw", (64, 1024), BF16)
        sbuf("gw", (128, 64), BF16)
        sbuf("b2_b", (8, 1024), BF16)
        sbuf("dd_b", (128, 1024), BF16)
        sbuf("cdb", (128, 8), F32)

        ps = {}
        for b in range(8):
            ps["P%d" % b] = En(nc.psum_tensor("P%d" % b, [128, 512], F32))

        sems = {}
        for e in ENGINES:
            sems[ENG_SEM[e]] = En(nc.semaphore(ENG_SEM[e]))
        for s in ("dx", "din", "din1", "dcd", "dcd1", "dw1", "dw1b", "dw2",
                  "dw2b", "dout", "dout1", "dsm", "dcast", "dbc", "dst", "dws",
                  "dbb0", "dbb1", "dbb2", "dbb3", "dge0", "dge1", "dcast0", "dcast1",
                  "dcast2", "dcast3", "dcast4", "dcast5", "dcast6",
                  "dsm1", "dsm2", "dsm3", "dsm4", "dsm5"):
            sems[s] = En(nc.semaphore(s))

        S = Sched()
        global LAST_SCHED
        LAST_SCHED = S
        S.record = [] if RECORD else None

        def apv(t_handle, offset, dims):
            base = t_handle[:, :] if t_handle.shape[0] == 128 else t_handle[0:t_handle.shape[0], :]
            return bass.AP(base.tensor, offset, [list(base.ap[0])] + [list(d) for d in dims])

        bank_open = {}

        def mm(out_ap, lhsT, rhs, bank, first, last, reads, tag=""):
            def fn(eng, out_ap=out_ap, lhsT=lhsT, rhs=rhs, first=first, last=last):
                return eng.matmul(out_ap, lhsT, rhs, start=first, stop=last,
                                  skip_group_check=True)
            S.add("pe", fn, reads=tuple(reads), writes=(bank,), tag=tag)

        def load(name_sb, src_ap, key, region=None, dma="dsm", engine="sp"):
            dst = sb[name_sb][:, :] if region is None else region
            def fn(eng, dst=dst, src=src_ap):
                return eng.dma_start(out=dst, in_=src)
            S.add(engine, fn, reads=(), writes=(key,), dma=dma, tag="ld:" + key)

        def fx0(eng):
            dst = apv(sb["x_f"], 0, [[258, 8], [1, min(T + 2, L)]])
            return eng.dma_start(out=dst, in_=xre[:, :, 0:min(T + 2, L)])
        S.add("sp", fx0, reads=(), writes=("x:p0",), dma="dx", tag="ldx0")
        load("ones1", ap["cones1"], "ones1", region=sb["ones1"][0:1, :], dma="dsm1")
        load("ident_f", ap["cident"], "ident_f", dma="dsm2")
        load("ident_b", ap["cidentb"], "ident_b", dma="dsm3")
        load("onescol", ap["conesc"], "onescol", dma="dsm4")
        load("dtb", ap["aw_dtb"], "dtb", dma="dsm5")
        load("b1c", ap["aw_b1"], "b1c", dma="dsm2")
        load("bs1c", ap["aw_bs1"], "bs1c", dma="dsm3")
        load("bs2c", ap["aw_bs2"], "bs2c", dma="dsm4")
        load("rs_c", ap["aw_rs"], "rs_c", dma="dsm5")

        def cast_small(src_name, dst_name, dst_region, cols, key):
            load("stage", ap[src_name], "stage:0",
                 region=sb["stage"][0:ap[src_name].shape[0], 0:cols])
            def fn(eng, dst=dst_region, src=sb["stage"][0:ap[src_name].shape[0], 0:cols]):
                return eng.activation(dst, src, AF.Copy)
            S.add("act", fn, reads=("stage:0",), writes=(key,), tag="cast:" + key)

        S.add("gp", lambda eng: eng.memset(sb["xzh"][:, :], 0.0),
              reads=(), writes=("xzh",), tag="xzh0")
        S.add("gp", lambda eng: eng.memset(sb["xzc"][:, :], 0.0),
              reads=(), writes=("xzc",), tag="xzc0")
        CONSTS = [1e-5, 1.0, -0.5, -1.0, 0.5]
        for k, v in enumerate(CONSTS):
            nc.const_aps.register(F32, v, sb["cdb"][:, k:k + 1])
            def fmc(eng, k=k, v=v):
                return eng.memset(sb["cdb"][:, k:k + 1], v)
            S.add("gp", fmc, reads=(), writes=("cdb",), tag="cdb%d" % k)

        def cast_blob(ci, nm_src, nm_blob, reads=()):
            def fn(eng, src=ap[nm_src], dst=blobs[nm_blob]):
                return eng.dma_start(out=dst, in_=src)
            S.add("gp", fn, reads=tuple(reads), writes=(nm_blob,),
                  dma="dcast%d" % ci, tag="cast:" + nm_blob)
        cast_blob(0, "w_in", "blob_in")
        cast_blob(1, "w_cd", "blob_cd")

        def late_setup():
            load("stage", ap["aw_alog"], "stage:0", region=sb["stage"][:, 0:128])
            S.add("act", lambda eng: eng.activation(sb["A_neg"][:, :], sb["stage"][:, 0:128], AF.Exp),
                  reads=("stage:0",), writes=("A_neg",), tag="Aexp")
            S.add("dve", lambda eng: eng.tensor_scalar(sb["A_neg"][:, :], sb["A_neg"][:, :],
                                                       -1.0, None, ALU.mult),
                  reads=("A_neg",), writes=("A_neg",), tag="Aneg")
            cast_small("aw_gw", "gw", sb["gw"][:, :], 64, "gw")
            cast_small("aw_dtw", "dtw", sb["dtw"][0:64, :], 1024, "dtw")
            cast_small("aw_xpw", "xpw", sb["xpw"][:, :], 768, "xpw")
            cast_small("aw_b2", "b2_b", sb["b2_b"][0:8, :], 1024, "b2_b")
            cast_small("aw_dd", "dd_b", sb["dd_b"][:, :], 1024, "dd_b")
            S.add("dve", lambda eng: eng.tensor_scalar(sb["rsh"][:, :], sb["rs_c"][:, :],
                                                       0.5, None, ALU.mult),
                  reads=("rs_c",), writes=("rsh",), tag="rsh")

        def late_setup_b():
            cast_blob(2, "w_w1", "blob_w1")
            cast_blob(3, "w_w2", "blob_w2")
            cast_blob(4, "w_out", "blob_out")
            cast_blob(5, "w_ws1", "blob_ws1")
            cast_blob(6, "w_ws2", "blob_ws2")
            def fnws1(eng):
                return eng.dma_start(out=sb["ws1_b"][:, :].rearrange("p (kc h) -> p kc h", kc=8),
                                     in_=blobs["blob_ws1"].rearrange("kc p h -> p kc h"))
            S.add("sp", fnws1, reads=("blob_ws1",), writes=("ws1_b",), dma="dws", tag="ld:ws1")
            def fnws2(eng):
                return eng.dma_start(out=sb["ws2_b"][:, :].rearrange("p (kc d) -> p kc d", kc=4),
                                     in_=blobs["blob_ws2"].rearrange("kc p d -> p kc d"))
            S.add("sp", fnws2, reads=("blob_ws2",), writes=("ws2_b",), dma="dws", tag="ld:ws2")

        xre = ap["x_t"].rearrange("(g p) t -> p g t", p=128)

        def prefetch(i):
            par = i % 2
            c0 = i * T
            wld = min(T + 2, L - c0)
            def fx(eng, c0=c0, wld=wld, par=par):
                dst = apv(sb["x_f"], par * 2064, [[258, 8], [1, wld]])
                return eng.dma_start(out=dst, in_=xre[:, :, c0:c0 + wld])
            S.add("sp", fx, reads=(), writes=("x:p%d" % par,), dma="dx", tag="ldx%d" % i)

        def prefetch_out(i):
            for kcp in range(8):
                def f(eng, kcp=kcp, par=kcp % 2):
                    dst = sb["sb_out"][:, par * 2048:(par + 1) * 2048].rearrange(
                        "p (kc d) -> p kc d", kc=2)
                    return eng.dma_start(out=dst, in_=blobs["blob_out"][kcp * 2:(kcp + 1) * 2, :, :].rearrange("kc p d -> p kc d"))
                S.add("sp", f, reads=("blob_out",), writes=("sbout:p%d" % (kcp % 2),),
                      dma="dout", tag="ldout%d.%d" % (i, kcp))

        def ln_stats(i):
            par = i % 2
            xk = "x:p%d" % par
            xoff = par * 2064
            if i == NCH - 1:
                def f(eng, xoff=xoff):
                    return eng.memset(apv(sb["x_f"], xoff + 256, [[258, 8], [1, 2]]), 0.0)
                S.add("dve", f, reads=(), writes=(xk,), tag="xpad%d" % i)
            def fxb(eng, xoff=xoff):
                return eng.tensor_copy(sb["scrb1"][:, 0:2064], apv(sb["x_f"], xoff, [[1, 2064]]))
            S.add("dve", fxb, reads=(xk,), writes=("scrb1:p0", "scrb1:p1"), tag="xb%d" % i)
            def fxq(eng, xoff=xoff):
                return eng.activation(sb["scrb2"][:, 0:2064], apv(sb["x_f"], xoff, [[1, 2064]]),
                                      AF.Square)
            S.add("act", fxq, reads=(xk,), writes=("scrb2:p0", "scrb2:p1"), tag="xsq%d" % i)
            for g in range(GD):
                mm(ps["P0"][0:1, 0:258], sb["onescol"][:, :],
                   sb["scrb1"][:, g * 258:(g + 1) * 258], "P0", g == 0, g == 7,
                   ["scrb1:p0", "scrb1:p1", "onescol"], tag="lnsum%d" % i)
            def fm(eng):
                return eng.tensor_scalar(sb["strow"][0:1, 516:774], ps["P0"][0:1, 0:258],
                                         1.0 / 1024.0, None, ALU.mult)
            S.add("dve", fm, reads=("P0",), writes=("strow_m",), tag="lnm%d" % i)
            for g in range(GD):
                mm(ps["P1"][0:1, 0:258], sb["onescol"][:, :],
                   sb["scrb2"][:, g * 258:(g + 1) * 258], "P1", g == 0, g == 7,
                   ["scrb2:p0", "scrb2:p1", "onescol"], tag="lnssq%d" % i)
            def fv1(eng):
                return eng.tensor_scalar(sb["strow"][0:1, 774:1032], ps["P1"][0:1, 0:258],
                                         1.0 / 1024.0, None, ALU.mult)
            S.add("dve", fv1, reads=("P1",), writes=("strow_v",), tag="lnv1%d" % i)
            def fmsq(eng):
                return eng.tensor_tensor(sb["strow"][0:1, 0:258], sb["strow"][0:1, 516:774],
                                         sb["strow"][0:1, 516:774], ALU.mult)
            S.add("dve", fmsq, reads=("strow_m",), writes=("strow_a",), tag="lnmsq%d" % i)
            def fv2(eng):
                return eng.tensor_tensor(sb["strow"][0:1, 774:1032], sb["strow"][0:1, 774:1032],
                                         sb["strow"][0:1, 0:258], ALU.subtract)
            S.add("dve", fv2, reads=("strow_v", "strow_a"), writes=("strow_v",), tag="lnv2%d" % i)
            def fln(eng):
                return eng.activation(sb["strow"][0:1, 774:1032], sb["strow"][0:1, 774:1032],
                                      AF.Ln, bias=1e-5)
            S.add("act", fln, reads=("strow_v", "cdb"), writes=("strow_v",), tag="lnln%d" % i)
            def fex(eng):
                return eng.activation(sb["strow"][0:1, 0:258], sb["strow"][0:1, 774:1032],
                                      AF.Exp, scale=-0.5)
            S.add("act", fex, reads=("strow_v", "cdb"), writes=("strow_a",), tag="lnex%d" % i)
            def fb1(eng):
                return eng.tensor_tensor(sb["strow"][0:1, 258:516], sb["strow"][0:1, 0:258],
                                         sb["strow"][0:1, 516:774], ALU.mult)
            S.add("dve", fb1, reads=("strow_a", "strow_m"), writes=("strow_b",), tag="lnb1%d" % i)
            def fb2(eng):
                return eng.tensor_scalar(sb["strow"][0:1, 258:516], sb["strow"][0:1, 258:516],
                                         -1.0, None, ALU.mult)
            S.add("dve", fb2, reads=("strow_b",), writes=("strow_b",), tag="lnb2%d" % i)
            mm(ps["P0"][:, 0:258], sb["ones1"][0:1, :], sb["strow"][0:1, 0:258],
               "P0", True, True, ["ones1", "strow_a"], tag="lnbca%d" % i)
            def fea(eng):
                return eng.activation(sb["lnab"][:, 0:258], ps["P0"][:, 0:258], AF.Copy)
            S.add("act", fea, reads=("P0",), writes=("lnab_a",), tag="lnea%d" % i)
            mm(ps["P1"][:, 0:258], sb["ones1"][0:1, :], sb["strow"][0:1, 258:516],
               "P1", True, True, ["ones1", "strow_b"], tag="lnbcb%d" % i)
            def feb(eng):
                return eng.activation(sb["lnab"][:, 258:516], ps["P1"][:, 0:258], AF.Copy)
            S.add("act", feb, reads=("P1",), writes=("lnab_b",), tag="lneb%d" % i)
        def ln_xn(i):
            par = i % 2
            xk = "x:p%d" % par
            xoff = par * 2064
            a_bc = apv(sb["lnab"], 0, [[0, 8], [1, 258]])
            b_bc = apv(sb["lnab"], 258, [[0, 8], [1, 258]])
            skeys = tuple("s:%d" % j for j in range(8))
            def fx1(eng, xoff=xoff, a_bc=a_bc):
                return eng.tensor_tensor(apv(sb["s_f"], 0, [[258, 8], [1, 258]]),
                                         apv(sb["x_f"], xoff, [[258, 8], [1, 258]]),
                                         a_bc, ALU.mult)
            S.add("dve", fx1, reads=(xk, "lnab_a"), writes=skeys, tag="xn1_%d" % i)
            def fx2(eng, b_bc=b_bc):
                return eng.tensor_tensor(apv(sb["xn_b"], 0, [[258, 8], [1, 258]]),
                                         apv(sb["s_f"], 0, [[258, 8], [1, 258]]),
                                         b_bc, ALU.add)
            S.add("dve", fx2, reads=skeys + ("lnab_b",), writes=("xn",), tag="xn2_%d" % i)

        def out_proj_phase_a(i):
            for kcp in (0, 1, 2, 4, 5, 6, 7):
                par = kcp % 2
                def fld(eng, kcp=kcp, par=par):
                    dst = sb["sb_out"][:, par * 2048:(par + 1) * 2048].rearrange(
                        "p (kc d) -> p kc d", kc=2)
                    return eng.dma_start(out=dst, in_=blobs["blob_out"][kcp * 2:(kcp + 1) * 2, :, :].rearrange("kc p d -> p kc d"))
                S.add("sp", fld, reads=("blob_out",), writes=("sbout:p%d" % par,),
                      dma="dout" if kcp % 2 == 0 else "dout1", tag="ldout%d.%d" % (i, kcp))
                for kci in range(2):
                    kc = kcp * 2 + kci
                    for j in range(GD):
                        bank = "P%d" % (4 + j // 2)
                        half = (j % 2) * 256
                        lhsT = sb["sb_out"][:, par * 2048 + kci * 1024 + j * 128:
                                            par * 2048 + kci * 1024 + (j + 1) * 128]
                        if kc < 8:
                            rhs = sb["y_b"][:, kc * 256:(kc + 1) * 256]
                            rk = "y"
                        else:
                            rhs = sb["zc"][:, (kc - 8) * 256:(kc - 7) * 256]
                            rk = "zc"
                        mm(ps[bank][:, half:half + 256], lhsT, rhs, bank,
                           (j % 2 == 0) and kc == 0, False,
                           ["sbout:p%d" % par, rk], tag="op%d.%d.%d" % (i, j, kc))
        def out_proj_a_evac(i, js):
            for j in js:
                bank = "P%d" % (4 + j // 2)
                half = (j % 2) * 256
                def fe(eng, j=j, bank=bank, half=half):
                    return eng.activation(sb["s_f"][:, j * 256:(j + 1) * 256],
                                          ps[bank][:, half:half + 256], AF.Copy)
                S.add("act", fe, reads=(bank,), writes=("s:%d" % j,), tag="ope%d.%d" % (i, j))

        def out_proj_phase_b(i):
            for kcp in (3,):
                par = kcp % 2
                def fld(eng, kcp=kcp, par=par):
                    dst = sb["sb_out"][:, par * 2048:(par + 1) * 2048].rearrange(
                        "p (kc d) -> p kc d", kc=2)
                    return eng.dma_start(out=dst, in_=blobs["blob_out"][kcp * 2:(kcp + 1) * 2, :, :].rearrange("kc p d -> p kc d"))
                S.add("sp", fld, reads=("blob_out",), writes=("sbout:p%d" % par,),
                      dma="dout" if kcp % 2 == 0 else "dout1", tag="ldoutb%d.%d" % (i, kcp))
                for kci in range(2):
                    kc = kcp * 2 + kci
                    for j in range(GD):
                        bank = "P%d" % (4 + j // 2)
                        half = (j % 2) * 256
                        lhsT = sb["sb_out"][:, par * 2048 + kci * 1024 + j * 128:
                                            par * 2048 + kci * 1024 + (j + 1) * 128]
                        rhs = sb["y_b"][:, kc * 256:(kc + 1) * 256]
                        mm(ps[bank][:, half:half + 256], lhsT, rhs, bank,
                           (j % 2 == 0) and kci == 0, False,
                           ["sbout:p%d" % par, "y"], tag="opb%d.%d.%d" % (i, j, kc))
            for q in range(4):
                bank = "P%d" % (4 + q)
                def fa(eng, q=q, bank=bank):
                    v = sb["s_f"][:, q * 512:(q + 1) * 512]
                    return eng.tensor_tensor(v, v, ps[bank][:, 0:512], ALU.add)
                S.add("dve", fa, reads=(bank, "s:%d" % (2 * q), "s:%d" % (2 * q + 1)),
                      writes=("s:%d" % (2 * q), "s:%d" % (2 * q + 1)), tag="opa%d.%d" % (i, q))

        def combine_phase(i):
            par = i % 2
            def f1(eng):
                return eng.tensor_tensor(sb["g_f"][:, :], sb["s_f"][:, 0:2048],
                                         sb["g_f"][:, :], ALU.mult)
            S.add("dve", f1, reads=tuple(["s:%d" % j for j in range(8)]) + ("g",),
                  writes=("g",), tag="cmb1_%d" % i)
            def f2(eng, par=par):
                return eng.tensor_tensor(sb["s_f"][:, 0:2048], sb["g_f"][:, :],
                                         apv(sb["x_f"], par * 2064, [[258, 8], [1, 256]]),
                                         ALU.add)
            S.add("dve", f2, reads=("g", "x:p%d" % par) + tuple(["s:%d" % j for j in range(8)]),
                  writes=tuple(["s:%d" % j for j in range(8)]), tag="cmb2_%d" % i)

        def store_phase(i):
            for tb in range(2):
                for gq in range(2):
                    bank = "P6" if gq == 0 else "P7"
                    for gg in range(4):
                        g = gq * 4 + gg
                        def ft(eng, g=g, gg=gg, tb=tb, bank=bank):
                            return eng.matmul(
                                bass.AP(ps[bank][:, :].tensor, gg * 128, [list(ps[bank][:, :].ap[0]), [1, 128]]),
                                sb["s_f"][:, g * 256 + tb * 128: g * 256 + (tb + 1) * 128],
                                sb["ident_f"][:, :], start=(gg == 0), stop=(gg == 3),
                                is_transpose=True, skip_group_check=True)
                        S.add("pe", ft, reads=["s:%d" % g, "ident_f"], writes=(bank,),
                              tag="tr%d.%d.%d" % (i, tb, g))
                    def fe(eng, gq=gq, tb=tb, bank=bank):
                        return eng.activation(sb["stage"][:, gq * 512:(gq + 1) * 512],
                                              ps[bank][:, :], AF.Copy)
                    S.add("act", fe, reads=(bank,), writes=("stage:0",),
                          tag="tre%d.%d.%d" % (i, tb, gq))
                def fs(eng, tb=tb, i=i):
                    return eng.dma_start(
                        out=ap["out"][i * T + tb * 128: i * T + (tb + 1) * 128, :],
                        in_=sb["stage"][:, 0:1024])
                S.add("sp", fs, reads=("stage:0",), writes=(), dma="dst",
                      tag="st%d.%d" % (i, tb))

        def in_proj_phase(i, jh):
            if jh == 0:
                def fc(eng):
                    dst = apv(sb["xzh"], 0, [[259, 16], [1, 3]])
                    return eng.tensor_copy(dst, sb["xzc"][:, :].rearrange("p (b c) -> p b c", b=16))
                S.add("dve", fc, reads=("xzc",), writes=("xzh",), tag="xzc%d" % i)
            for kc in range(8):
                par = (jh * 8 + kc) % 4
                def fld(eng, kc=kc, jh=jh, par=par):
                    dst = sb["sb_in"][:, par * 1024:(par + 1) * 1024]
                    return eng.dma_start(out=dst, in_=blobs["blob_in"][kc, :, jh * 1024:(jh + 1) * 1024])
                S.add("sp", fld, reads=("blob_in",), writes=("sbin:p%d" % par,),
                      dma="din" if kc % 2 == 0 else "din1",
                      tag="ldin%d.%d.%d" % (i, kc, jh))
                for j in range(8):
                    jg = jh * 8 + j
                    bank = "P%d" % (j // 2)
                    quarter = (j % 2) * 256
                    lhsT = sb["sb_in"][:, par * 1024 + j * 128: par * 1024 + (j + 1) * 128]
                    rhs = apv(sb["xn_b"], kc * 258 + 2, [[1, 256]])
                    mm(ps[bank][:, quarter:quarter + 256], lhsT, rhs, bank,
                       kc == 0 and (j % 2 == 0), False,
                       ["sbin:p%d" % par, "xn"], tag="ip%d.%d.%d.%d" % (i, jh, kc, j))
                if i == 0:
                    for j in range(8):
                        jg = jh * 8 + j
                        lhsT = sb["sb_in"][:, par * 1024 + j * 128: par * 1024 + (j + 1) * 128]
                        rhs = apv(sb["xn_b"], kc * 258, [[1, 2]])
                        mm(ps["P7"][:, jg * 2:(jg + 1) * 2], lhsT, rhs, "P7",
                           kc == 0 and jg == 0 and jh == 0, False,
                           ["sbin:p%d" % par, "xn"], tag="iph%d.%d" % (kc, jg))
            for b in range(4):
                j0 = jh * 8 + b * 2
                def fe(eng, b=b, j0=j0):
                    dst = apv(sb["xzh"], j0 * 259 + 3, [[259, 2], [1, 256]])
                    return eng.activation(dst, ps["P%d" % b][:, :].rearrange(
                        "p (b2 t) -> p b2 t", b2=2), AF.Copy)
                S.add("act", fe, reads=("P%d" % b,), writes=("xzh",),
                      tag="ipe%d.%d.%d" % (i, jh, b))
            if i == 0:
                def feh(eng, jh=jh):
                    dst = apv(sb["xzh"], jh * 8 * 259 + 1, [[259, 8], [1, 2]])
                    return eng.activation(dst, ps["P7"][:, jh * 16:jh * 16 + 16].rearrange(
                        "p (b2 t) -> p b2 t", b2=8), AF.Copy)
                S.add("act", feh, reads=("P7",), writes=("xzh",), tag="ipeh%d" % jh)
            if i == NCH - 1:
                def fz(eng, jh=jh):
                    return eng.memset(apv(sb["xzh"], jh * 8 * 259 + 257, [[259, 8], [1, 2]]), 0.0)
                S.add("dve", fz, reads=("xzh",), writes=("xzh",), tag="xzz%d.%d" % (i, jh))

        def conv_phase(i, br0):
            for br in (br0,):
                def ld(k):
                    par = (br * 4 + k) % 2
                    def fld(eng, br=br, k=k, par=par):
                        dst = sb["sb_cd"][:, par * 1024:(par + 1) * 1024]
                        return eng.dma_start(out=dst, in_=blobs["blob_cd"][br, k, :, :])
                    S.add("sp", fld, reads=("blob_cd",), writes=("sbcd:p%d" % par,),
                          dma="dcd" if k % 2 == 0 else "dcd1", tag="ldcd%d.%d.%d" % (i, br, k))
                ld(0)
                ld(1)
                for k in range(4):
                    par = (br * 4 + k) % 2
                    for g in range(GD):
                        bank = "P%d" % (g // 2)
                        half = (g % 2) * 256
                        blk = br * 8 + g
                        lhsT = sb["sb_cd"][:, par * 1024 + g * 128: par * 1024 + (g + 1) * 128]
                        rhs = apv(sb["xzh"], blk * 259 + k, [[1, 256]])
                        mm(ps[bank][:, half:half + 256], lhsT, rhs, bank,
                           k == 0 and (g % 2 == 0), False,
                           ["sbcd:p%d" % par, "xzh"], tag="cv%d.%d.%d.%d" % (i, br, k, g))
                    if k + 2 < 4:
                        ld(k + 2)
                for b in range(4):
                    if br == 0:
                        if b < 2:
                            dst = sb["xc"][:, b * 512:(b + 1) * 512]
                            wkey = "xc"
                        else:
                            par = i % 2
                            dst = sb["xco"][:, par * 1024 + (b - 2) * 512:
                                            par * 1024 + (b - 1) * 512]
                            wkey = "xco:p%d" % par
                    else:
                        dst = sb["zc"][:, b * 512:(b + 1) * 512]
                        wkey = "zc"
                    def fe(eng, b=b, dst=dst):
                        return eng.activation(dst, ps["P%d" % b][:, :], AF.Silu)
                    S.add("act", fe, reads=("P%d" % b,), writes=(wkey,),
                          tag="cve%d.%d.%d" % (i, br, b))

        def xz_tail_save(i, br0):
            def fts(eng, br0=br0):
                srcv = apv(sb["xzh"], br0 * 8 * 259 + 256, [[259, 8], [1, 3]])
                return eng.tensor_copy(
                    bass.AP(sb["xzc"][:, :].tensor, br0 * 24, [list(sb["xzc"][:, :].ap[0]), [3, 8], [1, 3]]),
                    srcv)
            S.add("dve", fts, reads=("xzh",), writes=("xzc",), tag="xzts%d.%d" % (i, br0))

        def xproj_phase(i):
            for kc in range(8):
                lhsT = sb["xpw"][:, kc * 96:(kc + 1) * 96]
                if kc < 4:
                    rhs = sb["xc"][:, kc * 256:(kc + 1) * 256]
                    rkey = "xc"
                else:
                    par = i % 2
                    rhs = sb["xco"][:, par * 1024 + (kc - 4) * 256:
                                    par * 1024 + (kc - 3) * 256]
                    rkey = "xco:p%d" % par
                mm(ps["P4"][0:96, 0:256], lhsT, rhs, "P4", kc == 0, kc == 7,
                   ["xpw", rkey], tag="xp%d.%d" % (i, kc))
            def f1(eng):
                return eng.activation(sb["dtr"][0:64, :], ps["P4"][0:64, 0:256], AF.Copy)
            S.add("act", f1, reads=("P4",), writes=("dtr",), tag="xpe1_%d" % i)
            def f2(eng):
                return eng.activation(sb["stage"][0:32, 768:1024], ps["P4"][64:96, 0:256], AF.Copy)
            S.add("act", f2, reads=("P4", "stage:0"), writes=("stage:0",), tag="xpe2_%d" % i)
            def f3(eng):
                return eng.activation(sb["bcB"][0:16, :], sb["stage"][0:16, 768:1024], AF.Copy)
            S.add("act", f3, reads=("stage:0",), writes=("bcB_B",), tag="xpe3_%d" % i)
            def f4(eng):
                return eng.dma_start(out=sb["bcB"][32:48, :], in_=sb["stage"][16:32, 768:1024])
            S.add("gp", f4, reads=("stage:0",), writes=("bcB_C",), dma="dbc", tag="xpe4_%d" % i)

        def dt_phase(i, js=None):
            for j in (range(GD) if js is None else js):
                bank = ("P4", "P5", "P6")[j % 3]
                half = 0
                lhsT = sb["dtw"][0:64, j * 128:(j + 1) * 128]
                mm(ps[bank][0:128, 0:256], lhsT, sb["dtr"][0:64, :], bank,
                   True, True, ["dtw", "dtr"], tag="dt%d.%d" % (i, j))
                scr = sb["stage"][:, (j % 4) * 256:(j % 4 + 1) * 256]
                def f1(eng, j=j, bank=bank, scr=scr):
                    return eng.activation(scr, ps[bank][:, 0:256], AF.Abs,
                                          bias=sb["dtb"][:, j:j + 1])
                S.add("act", f1, reads=(bank, "dtb", "stage:0"), writes=("stage:0",),
                      tag="dte1_%d.%d" % (i, j))
                def f4(eng, j=j, bank=bank):
                    return eng.activation(sb["delta"][:, j * 256:(j + 1) * 256],
                                          ps[bank][:, 0:256], AF.Relu,
                                          bias=sb["dtb"][:, j:j + 1])
                S.add("act", f4, reads=(bank, "dtb"), writes=("delta:%d" % j,), tag="dte4_%d.%d" % (i, j))
                def f2(eng, j=j, scr=scr):
                    return eng.activation(scr, scr, AF.Exp, scale=-1.0)
                S.add("act", f2, reads=("stage:0", "cdb"), writes=("stage:0",), tag="dte2_%d.%d" % (i, j))
                def f3(eng, j=j, scr=scr):
                    return eng.activation(scr, scr, AF.Ln, bias=1.0)
                S.add("act", f3, reads=("stage:0", "cdb"), writes=("stage:0",), tag="dte3_%d.%d" % (i, j))
                def f5(eng, j=j, scr=scr):
                    d = sb["delta"][:, j * 256:(j + 1) * 256]
                    return eng.tensor_tensor(d, d, scr, ALU.add)
                S.add("gp", f5, reads=("stage:0", "delta:%d" % j), writes=("delta:%d" % j,),
                      tag="dte5_%d.%d" % (i, j))
                if j < 4:
                    xsrc = sb["xc"][:, j * 256:(j + 1) * 256]
                    xkey = "xc"
                else:
                    par = i % 2
                    xsrc = sb["xco"][:, par * 1024 + (j - 4) * 256:
                                     par * 1024 + (j - 3) * 256]
                    xkey = "xco:p%d" % par
                def fu(eng, j=j, xsrc=xsrc):
                    return eng.tensor_tensor(sb["u_b"][:, j * 256:(j + 1) * 256],
                                             sb["delta"][:, j * 256:(j + 1) * 256],
                                             xsrc, ALU.mult)
                S.add("gp", fu, reads=("delta:%d" % j, xkey), writes=("u:%d" % j,), tag="u%d.%d" % (i, j))

        def gate_phase(i):
            for kc in range(8):
                lhsT = sb["gw"][:, kc * 8:(kc + 1) * 8]
                rhs = sb["xn_b"][:, kc * 258:kc * 258 + 256]
                mm(ps["P6"][0:8, 0:256], lhsT, rhs, "P6", kc == 0, kc == 7,
                   ["gw", "xn"], tag="gl%d.%d" % (i, kc))
            def ft(eng):
                return eng.activation(sb["gl_f"][0:8, :], ps["P6"][0:8, 0:256], AF.Tanh,
                                      scale=0.5)
            S.add("act", ft, reads=("P6", "cdb"), writes=("gl",), tag="gt%d" % i)
            for tb in range(2):
                def f(eng, tb=tb):
                    return eng.matmul(
                        bass.AP(ps["P6"][:, :].tensor, 256 + tb * 8, [list(ps["P6"][:, :].ap[0]), [1, 8]]),
                        sb["gl_f"][0:8, tb * 128:(tb + 1) * 128], sb["ident_f"][0:8, 0:8],
                        start=(tb == 0), stop=(tb == 1), is_transpose=True,
                        skip_group_check=True)
                S.add("pe", f, reads=("gl", "ident_f"), writes=("P6",), tag="gtr%d.%d" % (i, tb))
            def fs(eng):
                return eng.tensor_scalar(sb["st_f"][:, :], ps["P6"][:, 256:272],
                                         sb["rsh"][:, :], sb["rsh"][:, :], ALU.mult, ALU.add)
            S.add("dve", fs, reads=("P6", "rsh"), writes=("st",), tag="gs%d" % i)
            for tb in range(2):
                def f8(eng, tb=tb):
                    return eng.max(sb["m8"][:, tb * 8:(tb + 1) * 8], sb["st_f"][:, tb * 8:(tb + 1) * 8])
                S.add("dve", f8, reads=("st",), writes=("m8",), tag="gm8_%d.%d" % (i, tb))
            def fd(eng):
                return eng.tensor_tensor(sb["den"][:, :], apv(sb["m8"], 0, [[8, 2]]),
                                         apv(sb["m8"], 1, [[8, 2]]), ALU.add)
            S.add("dve", fd, reads=("m8",), writes=("den",), tag="gden%d" % i)
            def fr(eng):
                return eng.reciprocal(sb["rden"][:, :], sb["den"][:, :])
            S.add("dve", fr, reads=("den",), writes=("rden",), tag="grec%d" % i)
            for tb in range(2):
                def fi(eng, tb=tb):
                    return eng.tensor_scalar(sb["ind"][:, tb * 8:(tb + 1) * 8],
                                             sb["st_f"][:, tb * 8:(tb + 1) * 8],
                                             sb["m8"][:, tb * 8 + 1:tb * 8 + 2], None,
                                             ALU.is_ge)
                S.add("dve", fi, reads=("st", "m8"), writes=("ind",), tag="gind%d.%d" % (i, tb))
                def fg1(eng, tb=tb):
                    return eng.tensor_scalar(sb["gt"][:, tb * 8:(tb + 1) * 8],
                                             sb["st_f"][:, tb * 8:(tb + 1) * 8],
                                             sb["rden"][:, tb:tb + 1], None, ALU.mult)
                S.add("dve", fg1, reads=("st", "rden"), writes=("gt",), tag="gg1_%d.%d" % (i, tb))
            def fg2(eng):
                return eng.tensor_tensor(sb["gt"][:, :], sb["gt"][:, :], sb["ind"][:, :],
                                         ALU.mult)
            S.add("dve", fg2, reads=("gt", "ind"), writes=("gt",), tag="gg2_%d" % i)
            for tb in range(2):
                def fb(eng, tb=tb):
                    return eng.matmul(
                        ps["P6"][0:8, tb * 128:(tb + 1) * 128],
                        sb["gt"][:, tb * 8:(tb + 1) * 8], sb["ident_f"][:, :],
                        start=(tb == 0), stop=(tb == 1), is_transpose=True,
                        skip_group_check=True)
                S.add("pe", fb, reads=("gt", "ident_f"), writes=("P6",), tag="gbk%d.%d" % (i, tb))
            def fe(eng):
                return eng.activation(sb["gcm"][0:8, :], ps["P6"][0:8, 0:256], AF.Copy)
            S.add("act", fe, reads=("P6",), writes=("gcm",), tag="gev%d" % i)


        def gate_bcast(i):
            def fgd(eng):
                return eng.dma_start(out=blobs["ge_dram"][0:8, :], in_=sb["gcm"][0:8, :])
            S.add("gp", fgd, reads=("gcm",), writes=("ged",), dma="dge0", tag="ged%d" % i)
            def fgb(eng):
                srcv = bass.AP(blobs["ge_dram"].tensor, 0, [[0, 128], [256, 8], [1, 256]])
                return eng.dma_start(out=sb["ge_all"][:, :].rearrange("p (n t) -> p n t", n=8),
                                     in_=srcv)
            S.add("gp", fgb, reads=("ged",), writes=("ge_all",), dma="dge1", tag="geb%d" % i)

        def bbcb_phase(i):
            def f0(eng):
                return eng.dma_start(out=blobs["bc_dram"][0:16, :], in_=sb["bcB"][0:16, :])
            S.add("gp", f0, reads=("bcB_B",), writes=("bcd",),
                  dma="dbb0", tag="bcd%d" % i)
            def f0c(eng):
                return eng.dma_start(out=blobs["bc_dram"][32:48, :], in_=sb["bcB"][32:48, :])
            S.add("gp", f0c, reads=("bcB_C",), writes=("bcd",),
                  dma="dbb3", tag="bcdc%d" % i)
            for which, base, wk in [(0, 0, "Bb"), (1, 32, "Cb")]:
                def fd(eng, base=base, wk=wk):
                    srcv = bass.AP(blobs["bc_dram"].tensor, base * 256,
                                   [[0, 128], [256, 16], [1, 256]])
                    return eng.dma_start(out=sb[wk][:, :].rearrange("p (n t) -> p n t", n=16),
                                         in_=srcv)
                S.add("gp", fd, reads=("bcd",), writes=(wk,),
                      dma="dbb%d" % (1 + which), tag="bc%d.%d" % (i, which))

        def scan_group(i, g):
            dpar = g % 2
            dko = dpar * 4096
            dk = "dBx:p%d" % dpar
            def fb(eng, g=g, dko=dko):
                u_bc = apv(sb["u_b"], g * 256, [[0, 16], [1, 256]])
                return eng.tensor_tensor(apv(sb["dBx"], dko, [[256, 16], [1, 256]]),
                                         u_bc, apv(sb["Bb"], 0, [[256, 16], [1, 256]]),
                                         ALU.mult)
            S.add("dve", fb, reads=("u:%d" % g, "Bb", dk), writes=(dk,), tag="dbx%d.%d" % (i, g))
            for n in range(DSTATE):
                slot = (g * DSTATE + n) % 8
                def fx(eng, g=g, n=n, slot=slot):
                    return eng.activation(sb["dA"][:, slot * 256:(slot + 1) * 256],
                                          sb["delta"][:, g * 256:(g + 1) * 256],
                                          AF.Exp, scale=sb["A_neg"][:, g * 16 + n:g * 16 + n + 1])
                S.add("act", fx, reads=("delta:%d" % g, "A_neg", "dA:s%d" % slot),
                      writes=("dA:s%d" % slot,), tag="ex%d.%d.%d" % (i, g, n))
                if n % 4 == 3:
                    q0 = n - 3
                    s0 = (g * DSTATE + q0) % 8
                    skeys = tuple("dA:s%d" % ((s0 + k) % 8) for k in range(4))
                    bkeys = skeys[1:]
                    if i > 0:
                        def fm(eng, g=g, q0=q0, s0=s0):
                            return eng.tensor_tensor(
                                sb["fixt"][:, 0:3],
                                apv(sb["dA"], (s0 + 1) * 256, [[256, 3], [1, 1]]),
                                apv(sb["carry"], g * 16 + q0 + 1, [[1, 3], [1, 1]]),
                                ALU.mult)
                        S.add("dve", fm, reads=bkeys + ("carry:g%d" % g, "fixt"),
                              writes=("fixt",), tag="sfm%d.%d.%d" % (i, g, q0))
                        def fa(eng, g=g, q0=q0, dko=dko):
                            v = apv(sb["dBx"], dko + (q0 + 1) * 256, [[256, 3], [1, 1]])
                            return eng.tensor_tensor(v, v, sb["fixt"][:, 0:3], ALU.add)
                        S.add("dve", fa, reads=("fixt", dk), writes=(dk,),
                              tag="sfa%d.%d.%d" % (i, g, q0))
                    def fz(eng, s0=s0):
                        return eng.memset(apv(sb["dA"], (s0 + 1) * 256, [[256, 3], [1, 1]]), 0.0)
                    S.add("dve", fz, reads=bkeys, writes=bkeys, tag="sfz%d.%d.%d" % (i, g, q0))
                    def fsc(eng, g=g, q0=q0, s0=s0, i=i, dko=dko):
                        init = 0.0 if i == 0 else sb["carry"][:, g * 16 + q0:g * 16 + q0 + 1]
                        return eng.tensor_tensor_scan(
                            sb["h_b"][:, q0 * 256:(q0 + 4) * 256],
                            apv(sb["dA"], s0 * 256, [[1, 1024]]),
                            apv(sb["dBx"], dko + q0 * 256, [[1, 1024]]),
                            init, ALU.mult, ALU.add)
                    S.add("dve", fsc, reads=skeys + (dk, "carry:g%d" % g, "h"),
                          writes=("h",) + skeys, tag="sc%d.%d.%d" % (i, g, q0))
            def fcs(eng, g=g):
                src = apv(sb["h_b"], 255, [[256, 16]])
                return eng.tensor_copy(sb["carry"][:, g * 16:(g + 1) * 16], src)
            S.add("dve", fcs, reads=("h",), writes=("carry:g%d" % g,), tag="cs%d.%d" % (i, g))
            def fhc(eng, dko=dko):
                return eng.tensor_tensor(apv(sb["dBx"], dko, [[256, 16], [1, 256]]),
                                         apv(sb["h_b"], 0, [[256, 16], [1, 256]]),
                                         apv(sb["Cb"], 0, [[256, 16], [1, 256]]), ALU.mult)
            S.add("dve", fhc, reads=("h", "Cb", dk), writes=(dk,), tag="hc%d.%d" % (i, g))

        def idents_phase(i, g, first=False):
            dpar = g % 2
            dko = dpar * 4096
            dk = "dBx:p%d" % dpar
            half = dpar * 256
            for n in range(DSTATE):
                mm(ps["P7"][:, half:half + 256], sb["ident_b"][:, :],
                   apv(sb["dBx"], dko + n * 256, [[1, 256]]), "P7",
                   n == 0 and (dpar == 0 or first), False, [dk, "ident_b"],
                   tag="yid%d.%d.%d" % (i, g, n))
            if g < 4:
                xsrc = sb["xc"][:, g * 256:(g + 1) * 256]
                xkey = "xc"
            else:
                par = i % 2
                xsrc = sb["xco"][:, par * 1024 + (g - 4) * 256:
                                 par * 1024 + (g - 3) * 256]
                xkey = "xco:p%d" % par
            mm(ps["P7"][:, half:half + 256], sb["dd_b"][:, g * 128:(g + 1) * 128],
               xsrc, "P7", False, True,
               ["dd_b", xkey], tag="ydd%d.%d" % (i, g))
            def fy(eng, g=g, half=half):
                return eng.activation(sb["y_b"][:, g * 256:(g + 1) * 256],
                                      ps["P7"][:, half:half + 256], AF.Copy)
            S.add("act", fy, reads=("P7",), writes=("y",), tag="ye%d.%d" % (i, g))

        def ws1_phase(i):
            for kc in range(8):
                for hg in range(4):
                    bank = "P4" if hg < 2 else "P5"
                    half = (hg % 2) * 256
                    lhsT = sb["ws1_b"][:, kc * 512 + hg * 128: kc * 512 + (hg + 1) * 128]
                    rhs = sb["xn_b"][:, kc * 258:kc * 258 + 256]
                    mm(ps[bank][:, half:half + 256], lhsT, rhs, bank,
                       kc == 0 and hg % 2 == 0, False, ["ws1_b", "xn"],
                       tag="ws1_%d.%d.%d" % (i, kc, hg))
            for hg in range(4):
                bank = "P4" if hg < 2 else "P5"
                half = (hg % 2) * 256
                def fe(eng, hg=hg, bank=bank, half=half):
                    return eng.activation(sb["hsh"][:, hg * 256:(hg + 1) * 256],
                                          ps[bank][:, half:half + 256], AF.Gelu,
                                          bias=sb["bs1c"][:, hg:hg + 1])
                S.add("act", fe, reads=(bank, "bs1c"), writes=("hsh",), tag="ws1e%d.%d" % (i, hg))

        def ffn_expert(i, e):
            par = e % 2
            for kh in range(2):
                spar = (e * 2 + kh) % 2
                def fld1(eng, e=e, kh=kh, spar=spar):
                    dst = sb["sb_w1"][:, spar * 2048:(spar + 1) * 2048].rearrange(
                        "p (kc h) -> p kc h", kc=4)
                    return eng.dma_start(out=dst, in_=blobs["blob_w1"][e, kh * 4:(kh + 1) * 4, :, :].rearrange("kc p h -> p kc h"))
                S.add("sp", fld1, reads=("blob_w1",), writes=("sbw1:p%d" % spar,),
                      dma="dw1" if spar == 0 else "dw1b", tag="ldw1%d.%d.%d" % (i, e, kh))
                for kci in range(4):
                    kc = kh * 4 + kci
                    for hg in range(4):
                        bank = "P4" if hg < 2 else "P5"
                        half = (hg % 2) * 256
                        lhsT = sb["sb_w1"][:, spar * 2048 + kci * 512 + hg * 128:
                                           spar * 2048 + kci * 512 + (hg + 1) * 128]
                        rhs = sb["xn_b"][:, kc * 258:kc * 258 + 256]
                        mm(ps[bank][:, half:half + 256], lhsT, rhs, bank,
                           kc == 0 and hg % 2 == 0, False,
                           ["sbw1:p%d" % spar, "xn"], tag="w1_%d.%d.%d.%d" % (i, e, kc, hg))
            for hg in range(4):
                bank = "P4" if hg < 2 else "P5"
                half = (hg % 2) * 256
                def fe(eng, e=e, hg=hg, bank=bank, half=half, par=par):
                    return eng.activation(sb["scrb2"][:, par * 1024 + hg * 256:
                                                      par * 1024 + (hg + 1) * 256],
                                          ps[bank][:, half:half + 256], AF.Gelu,
                                          bias=sb["b1c"][:, e * 4 + hg:e * 4 + hg + 1])
                S.add("act", fe, reads=(bank, "b1c"), writes=("scrb2:p%d" % par,),
                      tag="gel%d.%d.%d" % (i, e, hg))
        def hgate_phase(i, e):
            par = e % 2
            def fhg(eng, e=e, par=par):
                geb = apv(sb["ge_all"], e * 256, [[0, 4], [1, 256]])
                return eng.tensor_tensor(
                    apv(sb["scrb1"], par * 1024, [[256, 4], [1, 256]]),
                    apv(sb["scrb2"], par * 1024, [[256, 4], [1, 256]]), geb, ALU.mult)
            S.add("dve", fhg, reads=("scrb2:p%d" % par, "ge_all", "scrb1:p%d" % par),
                  writes=("scrb1:p%d" % par,), tag="hg%d.%d" % (i, e))
        def ffn_w2(i, e):
            par = e % 2
            for kh in range(2):
                spar = (e * 2 + kh) % 2
                def fld2(eng, e=e, kh=kh, spar=spar):
                    dst = sb["sb_w2"][:, spar * 2048:(spar + 1) * 2048].rearrange(
                        "p (kc d) -> p kc d", kc=2)
                    return eng.dma_start(out=dst, in_=blobs["blob_w2"][e, kh * 2:(kh + 1) * 2, :, :].rearrange("kc p d -> p kc d"))
                S.add("sp", fld2, reads=("blob_w2",), writes=("sbw2:p%d" % spar,),
                      dma="dw2" if spar == 0 else "dw2b", tag="ldw2%d.%d.%d" % (i, e, kh))
                for kci in range(2):
                    kc = kh * 2 + kci
                    for j in range(8):
                        bank = "P%d" % (j // 2)
                        half = (j % 2) * 256
                        lhsT = sb["sb_w2"][:, spar * 2048 + kci * 1024 + j * 128:
                                           spar * 2048 + kci * 1024 + (j + 1) * 128]
                        rhs = apv(sb["scrb1"], par * 1024 + kc * 256, [[1, 256]])
                        mm(ps[bank][:, half:half + 256], lhsT, rhs, bank,
                           e == 0 and kc == 0 and (j % 2 == 0), False,
                           ["sbw2:p%d" % spar, "scrb1:p%d" % par],
                           tag="w2_%d.%d.%d.%d" % (i, e, kc, j))

        def ffn_tail(i):
            for j in range(8):
                bank = "P%d" % (j // 2)
                half = (j % 2) * 256
                mm(ps[bank][:, half:half + 256], sb["b2_b"][0:8, j * 128:(j + 1) * 128],
                   sb["gcm"][0:8, :], bank, False, False, ["b2_b", "gcm"],
                   tag="b2_%d.%d" % (i, j))
            for kc in range(4):
                for j in range(8):
                    bank = "P%d" % (j // 2)
                    half = (j % 2) * 256
                    lhsT = sb["ws2_b"][:, kc * 1024 + j * 128: kc * 1024 + (j + 1) * 128]
                    rhs = sb["hsh"][:, kc * 256:(kc + 1) * 256]
                    mm(ps[bank][:, half:half + 256], lhsT, rhs, bank,
                       False, kc == 3, ["ws2_b", "hsh"], tag="ws2_%d.%d.%d" % (i, kc, j))
            for j in range(8):
                bank = "P%d" % (j // 2)
                half = (j % 2) * 256
                def fe(eng, j=j, bank=bank, half=half):
                    return eng.activation(sb["g_f"][:, j * 256:(j + 1) * 256],
                                          ps[bank][:, half:half + 256], AF.Silu,
                                          bias=sb["bs2c"][:, j:j + 1])
                S.add("act", fe, reads=(bank, "bs2c"), writes=("g",), tag="ge%d.%d" % (i, j))

        def dbg_dump(name, tensor_key, sbname, cast=True):
            def f(eng, name=name, sbname=sbname):
                return eng.dma_start(out=ap[name], in_=sb[sbname][:, 0:ap[name].shape[1]])
            S.add("gp", f, reads=(tensor_key,), writes=(), dma="dbc", tag="dbg:" + name)

        GSPLIT = 6
        ln_stats(0)
        late_setup()
        for i in range(NCH + 1):
            if i < NCH:
                ln_xn(i)
            if i >= 1:
                scan_group(i - 1, GSPLIT)
            if i < NCH:
                in_proj_phase(i, 0)
            if i >= 1:
                scan_group(i - 1, GSPLIT + 1)
                out_proj_phase_a(i - 1)
                out_proj_a_evac(i - 1, (6, 7, 4, 5, 0, 1, 2, 3))
            if i < NCH:
                conv_phase(i, 0)
            if i >= 1:
                idents_phase(i - 1, GSPLIT, first=True)
            if i < NCH:
                gate_phase(i)
                xproj_phase(i)
                bbcb_phase(i)
            if i < NCH:
                dt_phase(i)
            if i >= 1:
                idents_phase(i - 1, GSPLIT + 1, first=True)
            if i < NCH:
                gate_bcast(i)
                scan_group(i, 0)
                in_proj_phase(i, 1)
                scan_group(i, 1)
                conv_phase(i, 1)
                if i == 0:
                    late_setup_b()
                if DBG_CHUNK == i:
                    dbg_dump("dbg_xz", "xzh", "xzh")
                    dbg_dump("dbg_bb", "Bb", "Bb")
                    dbg_dump("dbg_cb", "Cb", "Cb")
                    dbg_dump("dbg_delta", "delta", "delta")
                    dbg_dump("dbg_u", "u", "u_b")
            if i < NCH:
                for g in range(GD):
                    ffn_expert(i, g)
                    if g < GSPLIT:
                        idents_phase(i, g)
                    if g >= 1:
                        ffn_w2(i, g - 1)
                    hgate_phase(i, g)
                    if g + 2 < GSPLIT:
                        scan_group(i, g + 2)
                ffn_w2(i, GD - 1)
                ws1_phase(i)
            if i >= 1:
                out_proj_phase_b(i - 1)
                combine_phase(i - 1)
                store_phase(i - 1)
            if i < NCH:
                ffn_tail(i)
                xz_tail_save(i, 0)
                xz_tail_save(i, 1)
                if i + 1 < NCH:
                    prefetch(i + 1)
                    ln_stats(i + 1)
                if DBG_CHUNK == i:
                    dbg_dump("dbg_gf", "g", "g_f")
                    dbg_dump("dbg_hg", "scrb1:p1", "scrb1")
                    dbg_dump("dbg_gel", "scrb2:p1", "scrb2")
                    dbg_dump("dbg_ge", "ge_all", "ge_all")

        S.resolve()

        eng_of = {"sp": "sync", "pe": "tensor", "act": "scalar",
                  "dve": "vector", "gp": "gpsimd"}

        with nc.Block() as block:
            def section(engine_name):
                def run(eng):
                    for op in S.ops:
                        if op["e"] != engine_name:
                            continue
                        for (s, v) in op["waits"]:
                            eng.wait_ge(sems[s], v)
                        inst = op["fn"](eng)
                        if getattr(S, "record", None) is not None:
                            try:
                                S.record.append((op["tag"], inst.ins.name))
                            except Exception:
                                pass
                        if op["inc"] is not None:
                            s, step = op["inc"]
                            inst.then_inc(sems[s], step)
                    if engine_name == "sp":
                        eng.wait_ge(sems["dst"], S.final.get("dst", 0))
                return run

            block.sync(section("sp"))
            block.tensor(section("pe"))
            block.scalar(section("act"))
            block.vector(section("dve"))
            block.gpsimd(section("gp"))

    return nc


def host_prep(inputs, L):
    f = np.float32
    W1 = np.asarray(inputs["L0_ffn_W1"], f)
    W2 = np.asarray(inputs["L0_ffn_W2"], f)
    d = {}
    d["w_in"] = np.ascontiguousarray(np.asarray(inputs["L0_attn_in_proj_w"], f).T).reshape(8, 128, 2048)
    d["w_out"] = np.ascontiguousarray(np.asarray(inputs["L0_attn_out_proj_w"], f).T).reshape(16, 128, 1024)
    d["w_w1"] = np.ascontiguousarray(W1.reshape(8, 8, 128, 512))
    d["w_w2"] = np.ascontiguousarray(W2.reshape(8, 4, 128, 1024))
    d["w_ws1"] = np.ascontiguousarray(np.asarray(inputs["L0_ffn_Ws1"], f).reshape(8, 128, 512))
    d["w_ws2"] = np.ascontiguousarray(np.asarray(inputs["L0_ffn_Ws2"], f).reshape(4, 128, 1024))
    cd = np.zeros((2, 4, 128, 1024), f)
    for br, wname in enumerate(["L0_attn_conv_x_w", "L0_attn_conv_z_w"]):
        wk = np.asarray(inputs[wname], f)[:, 0, :]
        for k in range(4):
            for g in range(8):
                idx = np.arange(128)
                cd[br, k, idx, g * 128 + idx] = wk[g * 128 + idx, k]
    d["w_cd"] = cd
    d["aw_alog"] = np.ascontiguousarray(
        np.asarray(inputs["L0_attn_A_log"], f).reshape(8, 128, 16).transpose(1, 0, 2).reshape(128, 128))
    d["aw_dtb"] = np.ascontiguousarray(np.asarray(inputs["L0_attn_dt_proj_b"], f).reshape(8, 128).T)
    dd = np.zeros((128, 1024), f)
    Dv = np.asarray(inputs["L0_attn_D"], f)
    for g in range(8):
        idx = np.arange(128)
        dd[idx, g * 128 + idx] = Dv[g * 128 + idx]
    d["aw_dd"] = dd
    b1 = np.asarray(inputs["L0_ffn_b1"], f)
    d["aw_b1"] = np.ascontiguousarray(b1.reshape(8, 4, 128).transpose(2, 0, 1).reshape(128, 32))
    d["aw_bs1"] = np.ascontiguousarray(np.asarray(inputs["L0_ffn_bs1"], f).reshape(4, 128).T)
    d["aw_bs2"] = np.ascontiguousarray(np.asarray(inputs["L0_ffn_bs2"], f).reshape(8, 128).T)
    d["aw_rs"] = np.full((128, 1), np.asarray(inputs["L0_ffn_route_scale"], f))
    d["aw_xpw"] = np.ascontiguousarray(
        np.asarray(inputs["L0_attn_x_proj_w"], f).T.reshape(8, 128, 96).transpose(1, 0, 2).reshape(128, 768))
    d["aw_dtw"] = np.ascontiguousarray(np.asarray(inputs["L0_attn_dt_proj_w"], f).T)
    d["aw_gw"] = np.ascontiguousarray(
        np.asarray(inputs["L0_ffn_gate_W"], f).reshape(8, 128, 8).transpose(1, 0, 2).reshape(128, 64))
    d["aw_b2"] = np.ascontiguousarray(np.asarray(inputs["L0_ffn_b2"], f))
    d["cident"] = np.eye(128, dtype=f)
    d["cidentb"] = np.eye(128, dtype=ml_dtypes.bfloat16)
    d["cones1"] = np.ones((1, 128), f)
    d["conesc"] = np.ones((128, 1), ml_dtypes.bfloat16)
    return d


def kernel(**inputs):
    x = np.asarray(inputs["L0_attn_hidden_states"], np.float32)
    B, L, Dm = x.shape
    assert Dm == D
    shared = host_prep(inputs, L)
    nc = build_nc(L)
    in_maps = []
    for b in range(B):
        m = dict(shared)
        m["x_t"] = np.ascontiguousarray(x[b].T)
        in_maps.append(m)
    res = run_bass_kernel_spmd(nc, in_maps, core_ids=list(range(B)))
    out = np.stack([np.asarray(res.results[b]["out"]) for b in range(B)], axis=0)
    return out.astype(np.float32)


# version 64
# speedup vs baseline: 1.1340x; 1.1340x over previous
import numpy as np
import ml_dtypes
import concourse.bass as bass
import concourse.mybir as mybir
from concourse.bass_utils import run_bass_kernel_spmd

F32 = mybir.dt.float32
BF16 = mybir.dt.bfloat16
AF = mybir.ActivationFunctionType
ALU = mybir.AluOpType

D = 1024
GD = 8
DSTATE = 16
DTRANK = 64
E = 8
T = 256

DBG_CHUNK = None
RECORD = False
LAST_SCHED = None

ENGINES = ("sp", "pe", "act", "dve", "gp")
ENG_SEM = {e: "c_" + e for e in ENGINES}


class Sched:
    def __init__(self):
        self.ops = []
        self.dma_sems = set()

    def add(self, engine, fn, reads=(), writes=(), dma=None, tag=""):
        if dma is not None:
            self.dma_sems.add(dma)
        self.ops.append(dict(e=engine, fn=fn, r=tuple(reads), w=tuple(writes),
                             dma=dma, tag=tag, waits=None, inc=None))

    def resolve(self):
        last_w = {}
        readers = {}
        dma_last = {}
        edges = [[] for _ in self.ops]
        for i, op in enumerate(self.ops):
            deps = set()
            for k in op["r"]:
                if k in last_w:
                    deps.add(last_w[k])
            for k in op["w"]:
                if k in last_w:
                    deps.add(last_w[k])
                for r in readers.get(k, ()):
                    deps.add(r)
            if op["dma"] is not None and op["dma"] in dma_last:
                deps.add(dma_last[op["dma"]])
            deps.discard(i)
            edges[i] = sorted(deps)
            for k in op["r"]:
                readers.setdefault(k, []).append(i)
            for k in op["w"]:
                last_w[k] = i
                readers[k] = []
            if op["dma"] is not None:
                dma_last[op["dma"]] = i

        need_inc = [False] * len(self.ops)
        for i, op in enumerate(self.ops):
            for j in edges[i]:
                p = self.ops[j]
                if p["dma"] is None and op["dma"] is None and \
                        p["e"] == "pe" and op["e"] == "pe":
                    continue
                need_inc[j] = True
        for i, op in enumerate(self.ops):
            if op["dma"] is not None:
                need_inc[i] = True

        ev = [None] * len(self.ops)
        eng_cnt = {e: 0 for e in ENGINES}
        dma_cnt = {}
        for i, op in enumerate(self.ops):
            if not need_inc[i]:
                continue
            if op["dma"] is not None:
                s = op["dma"]
                dma_cnt[s] = dma_cnt.get(s, 0) + 1
                ev[i] = (s, dma_cnt[s] * 16)
                op["inc"] = (s, 16)
            else:
                s = ENG_SEM[op["e"]]
                eng_cnt[op["e"]] += 1
                ev[i] = (s, eng_cnt[op["e"]])
                op["inc"] = (s, 1)
        clocks = {e: {} for e in ENGINES}
        for i, op in enumerate(self.ops):
            e = op["e"]
            waits = []
            for j in edges[i]:
                p = self.ops[j]
                if p["dma"] is None and op["dma"] is None and \
                        p["e"] == "pe" and op["e"] == "pe":
                    continue
                sem, val = ev[j]
                if clocks[e].get(sem, 0) >= val:
                    continue
                waits.append((sem, val))
                clocks[e][sem] = val
            op["waits"] = waits
        self.final = {("c_" + e): c for e, c in eng_cnt.items()}
        self.final.update({s: c * 16 for s, c in dma_cnt.items()})


def build_nc(L):
    NCH = L // T
    nc = bass.Bass("TRN2", target_bir_lowering=False)
    ap = {}

    def din(name, shape, dt=F32):
        ap[name] = nc.dram_tensor(name, list(shape), dt, kind="ExternalInput").ap()

    din("x_t", (D, L))
    ap["out"] = nc.dram_tensor("out", [L, D], F32, kind="ExternalOutput").ap()
    if DBG_CHUNK is not None:
        for nm, shape in [("dbg_gf", (128, 2048)), ("dbg_hg", (128, 2048)),
                          ("dbg_gel", (128, 2048)), ("dbg_ge", (128, 512)),
                          ("dbg_xz", (128, 16 * 259)), ("dbg_bb", (128, 4096)),
                          ("dbg_cb", (128, 4096)), ("dbg_h", (128, 4096)),
                          ("dbg_da", (128, 2048)), ("dbg_delta", (128, 2048)),
                          ("dbg_u", (128, 2048))]:
            ap[nm] = nc.dram_tensor(nm, list(shape), F32, kind="ExternalOutput").ap()

    din("w_in", (8, 128, 2048))
    din("w_out", (16, 128, 1024))
    din("w_w1", (8, 8, 128, 512))
    din("w_w2", (8, 4, 128, 1024))
    din("w_ws1", (8, 128, 512))
    din("w_ws2", (4, 128, 1024))
    din("w_cd", (2, 4, 128, 1024))
    din("aw_alog", (128, 128))
    din("aw_dtb", (128, 8))
    din("aw_dd", (128, 1024))
    din("aw_b1", (128, 32))
    din("aw_bs1", (128, 4))
    din("aw_bs2", (128, 8))
    din("aw_rs", (128, 1))
    din("aw_xpw", (128, 768))
    din("aw_dtw", (64, 1024))
    din("aw_gw", (128, 64))
    din("aw_b2", (8, 1024))
    din("cident", (128, 128))
    din("cidentb", (128, 128), BF16)
    din("cones1", (1, 128))
    din("conesc", (128, 1), BF16)

    blobs = {}
    blobs["bc_dram"] = nc.dram_tensor("bc_dram", [48, 256], BF16, kind="Internal").ap()
    blobs["ge_dram"] = nc.dram_tensor("ge_dram", [8, 256], BF16, kind="Internal").ap()
    for nm, shape in [("blob_in", (8, 128, 2048)), ("blob_out", (16, 128, 1024)),
                      ("blob_w1", (8, 8, 128, 512)), ("blob_w2", (8, 4, 128, 1024)),
                      ("blob_ws1", (8, 128, 512)), ("blob_ws2", (4, 128, 1024)),
                      ("blob_cd", (2, 4, 128, 1024))]:
        blobs[nm] = nc.dram_tensor(nm, list(shape), BF16, kind="Internal").ap()

    from contextlib import ExitStack
    with ExitStack() as ctx:
        En = ctx.enter_context
        sb = {}

        def sbuf(name, shape, dt):
            sb[name] = En(nc.sbuf_tensor(name, list(shape), dt))

        sbuf("x_f", (128, 2 * 8 * 258), F32)
        sbuf("xn_b", (128, 8 * 258), BF16)
        sbuf("scrb1", (128, 2080), BF16)
        sbuf("scrb2", (128, 2080), BF16)
        sbuf("lnab", (128, 516), F32)
        sbuf("strow", (1, 1032), F32)
        sbuf("xzh", (128, 16 * 259), BF16)
        sbuf("xc", (128, 1024), BF16)
        sbuf("xco", (128, 2048), BF16)
        sbuf("zc", (128, 2048), BF16)
        sbuf("dtr", (64, 256), BF16)
        sbuf("bcB", (48, 256), BF16)
        sbuf("delta", (128, 2080), F32)
        sbuf("u_b", (128, 2048), BF16)
        sbuf("Bb", (128, 4096), BF16)
        sbuf("Cb", (128, 4096), BF16)
        sbuf("xzc", (128, 48), BF16)
        sbuf("dA", (128, 2048), F32)
        sbuf("fixt", (128, 4), F32)
        sbuf("dBx", (128, 2 * 4096), BF16)
        sbuf("h_b", (128, 4096), BF16)
        sbuf("carry", (128, 128), F32)
        sbuf("y_b", (128, 2048), BF16)
        sbuf("s_f", (128, 2080), F32)
        sbuf("g_f", (128, 2048), F32)
        sbuf("hsh", (128, 1024), BF16)
        sbuf("gl_f", (8, 256), F32)
        sbuf("st_f", (128, 16), F32)
        sbuf("m8", (128, 16), F32)
        sbuf("den", (128, 2), F32)
        sbuf("rden", (128, 2), F32)
        sbuf("ind", (128, 16), F32)
        sbuf("gt", (128, 16), F32)
        sbuf("gcm", (8, 256), BF16)
        sbuf("ge_all", (128, 2048), BF16)
        sbuf("stage", (128, 1024), F32)
        sbuf("sb_in", (128, 4 * 1024), BF16)
        sbuf("sb_cd", (128, 2 * 1024), BF16)
        sbuf("sb_w1", (128, 2 * 2048), BF16)
        sbuf("sb_w2", (128, 2 * 2048), BF16)
        sbuf("sb_out", (128, 2 * 2048), BF16)
        sbuf("ws1_b", (128, 4096), BF16)
        sbuf("ws2_b", (128, 4096), BF16)
        sbuf("ident_f", (128, 128), F32)
        sbuf("ident_b", (128, 128), BF16)
        sbuf("ones1", (1, 128), F32)
        sbuf("onescol", (128, 1), BF16)
        sbuf("A_neg", (128, 128), F32)
        sbuf("dtb", (128, 8), F32)
        sbuf("b1c", (128, 32), F32)
        sbuf("bs1c", (128, 4), F32)
        sbuf("bs2c", (128, 8), F32)
        sbuf("rsh", (128, 1), F32)
        sbuf("rs_c", (128, 1), F32)
        sbuf("xpw", (128, 768), BF16)
        sbuf("dtw", (64, 1024), BF16)
        sbuf("gw", (128, 64), BF16)
        sbuf("b2_b", (8, 1024), BF16)
        sbuf("dd_b", (128, 1024), BF16)
        sbuf("cdb", (128, 8), F32)

        ps = {}
        for b in range(8):
            ps["P%d" % b] = En(nc.psum_tensor("P%d" % b, [128, 512], F32))

        sems = {}
        for e in ENGINES:
            sems[ENG_SEM[e]] = En(nc.semaphore(ENG_SEM[e]))
        for s in ("dx", "din", "din1", "dcd", "dcd1", "dw1", "dw1b", "dw2",
                  "dw2b", "dout", "dout1", "dsm", "dcast", "dbc", "dst", "dws",
                  "dbb0", "dbb1", "dbb2", "dbb3", "dge0", "dge1", "dcast0", "dcast1",
                  "dcast2", "dcast3", "dcast4", "dcast5", "dcast6",
                  "dsm1", "dsm2", "dsm3", "dsm4", "dsm5"):
            sems[s] = En(nc.semaphore(s))

        S = Sched()
        global LAST_SCHED
        LAST_SCHED = S
        S.record = [] if RECORD else None

        def apv(t_handle, offset, dims):
            base = t_handle[:, :] if t_handle.shape[0] == 128 else t_handle[0:t_handle.shape[0], :]
            return bass.AP(base.tensor, offset, [list(base.ap[0])] + [list(d) for d in dims])

        bank_open = {}

        def mm(out_ap, lhsT, rhs, bank, first, last, reads, tag=""):
            def fn(eng, out_ap=out_ap, lhsT=lhsT, rhs=rhs, first=first, last=last):
                return eng.matmul(out_ap, lhsT, rhs, start=first, stop=last,
                                  skip_group_check=True)
            S.add("pe", fn, reads=tuple(reads), writes=(bank,), tag=tag)

        def load(name_sb, src_ap, key, region=None, dma="dsm", engine="sp"):
            dst = sb[name_sb][:, :] if region is None else region
            def fn(eng, dst=dst, src=src_ap):
                return eng.dma_start(out=dst, in_=src)
            S.add(engine, fn, reads=(), writes=(key,), dma=dma, tag="ld:" + key)

        def fx0(eng):
            dst = apv(sb["x_f"], 0, [[258, 8], [1, min(T + 2, L)]])
            return eng.dma_start(out=dst, in_=xre[:, :, 0:min(T + 2, L)])
        S.add("sp", fx0, reads=(), writes=("x:p0",), dma="dx", tag="ldx0")
        load("ones1", ap["cones1"], "ones1", region=sb["ones1"][0:1, :], dma="dsm1")
        load("ident_f", ap["cident"], "ident_f", dma="dsm2")
        load("ident_b", ap["cidentb"], "ident_b", dma="dsm3")
        load("onescol", ap["conesc"], "onescol", dma="dsm4")
        load("dtb", ap["aw_dtb"], "dtb", dma="dsm5")
        load("b1c", ap["aw_b1"], "b1c", dma="dsm2")
        load("bs1c", ap["aw_bs1"], "bs1c", dma="dsm3")
        load("bs2c", ap["aw_bs2"], "bs2c", dma="dsm4")
        load("rs_c", ap["aw_rs"], "rs_c", dma="dsm5")

        def cast_small(src_name, dst_name, dst_region, cols, key):
            load("stage", ap[src_name], "stage:0",
                 region=sb["stage"][0:ap[src_name].shape[0], 0:cols])
            def fn(eng, dst=dst_region, src=sb["stage"][0:ap[src_name].shape[0], 0:cols]):
                return eng.activation(dst, src, AF.Copy)
            S.add("act", fn, reads=("stage:0",), writes=(key,), tag="cast:" + key)

        S.add("gp", lambda eng: eng.memset(sb["xzh"][:, :], 0.0),
              reads=(), writes=("xzh",), tag="xzh0")
        S.add("gp", lambda eng: eng.memset(sb["xzc"][:, :], 0.0),
              reads=(), writes=("xzc",), tag="xzc0")
        CONSTS = [1e-5, 1.0, -0.5, -1.0, 0.5]
        for k, v in enumerate(CONSTS):
            nc.const_aps.register(F32, v, sb["cdb"][:, k:k + 1])
            def fmc(eng, k=k, v=v):
                return eng.memset(sb["cdb"][:, k:k + 1], v)
            S.add("gp", fmc, reads=(), writes=("cdb",), tag="cdb%d" % k)

        def cast_blob(ci, nm_src, nm_blob, reads=()):
            def fn(eng, src=ap[nm_src], dst=blobs[nm_blob]):
                return eng.dma_start(out=dst, in_=src)
            S.add("gp", fn, reads=tuple(reads), writes=(nm_blob,),
                  dma="dcast%d" % ci, tag="cast:" + nm_blob)
        cast_blob(0, "w_in", "blob_in")
        cast_blob(1, "w_cd", "blob_cd")

        def late_setup():
            load("stage", ap["aw_alog"], "stage:0", region=sb["stage"][:, 0:128])
            S.add("act", lambda eng: eng.activation(sb["A_neg"][:, :], sb["stage"][:, 0:128], AF.Exp),
                  reads=("stage:0",), writes=("A_neg",), tag="Aexp")
            S.add("dve", lambda eng: eng.tensor_scalar(sb["A_neg"][:, :], sb["A_neg"][:, :],
                                                       -1.0, None, ALU.mult),
                  reads=("A_neg",), writes=("A_neg",), tag="Aneg")
            cast_small("aw_gw", "gw", sb["gw"][:, :], 64, "gw")
            cast_small("aw_dtw", "dtw", sb["dtw"][0:64, :], 1024, "dtw")
            cast_small("aw_xpw", "xpw", sb["xpw"][:, :], 768, "xpw")
            cast_small("aw_b2", "b2_b", sb["b2_b"][0:8, :], 1024, "b2_b")
            cast_small("aw_dd", "dd_b", sb["dd_b"][:, :], 1024, "dd_b")
            S.add("dve", lambda eng: eng.tensor_scalar(sb["rsh"][:, :], sb["rs_c"][:, :],
                                                       0.5, None, ALU.mult),
                  reads=("rs_c",), writes=("rsh",), tag="rsh")

        def late_setup_b():
            cast_blob(2, "w_w1", "blob_w1")
            cast_blob(3, "w_w2", "blob_w2")
            cast_blob(4, "w_out", "blob_out")
            cast_blob(5, "w_ws1", "blob_ws1")
            cast_blob(6, "w_ws2", "blob_ws2")
            def fnws1(eng):
                return eng.dma_start(out=sb["ws1_b"][:, :].rearrange("p (kc h) -> p kc h", kc=8),
                                     in_=blobs["blob_ws1"].rearrange("kc p h -> p kc h"))
            S.add("sp", fnws1, reads=("blob_ws1",), writes=("ws1_b",), dma="dws", tag="ld:ws1")
            def fnws2(eng):
                return eng.dma_start(out=sb["ws2_b"][:, :].rearrange("p (kc d) -> p kc d", kc=4),
                                     in_=blobs["blob_ws2"].rearrange("kc p d -> p kc d"))
            S.add("sp", fnws2, reads=("blob_ws2",), writes=("ws2_b",), dma="dws", tag="ld:ws2")

        xre = ap["x_t"].rearrange("(g p) t -> p g t", p=128)

        def prefetch(i):
            par = i % 2
            c0 = i * T
            wld = min(T + 2, L - c0)
            def fx(eng, c0=c0, wld=wld, par=par):
                dst = apv(sb["x_f"], par * 2064, [[258, 8], [1, wld]])
                return eng.dma_start(out=dst, in_=xre[:, :, c0:c0 + wld])
            S.add("sp", fx, reads=(), writes=("x:p%d" % par,), dma="dx", tag="ldx%d" % i)

        def prefetch_out(i):
            for kcp in range(8):
                def f(eng, kcp=kcp, par=kcp % 2):
                    dst = sb["sb_out"][:, par * 2048:(par + 1) * 2048].rearrange(
                        "p (kc d) -> p kc d", kc=2)
                    return eng.dma_start(out=dst, in_=blobs["blob_out"][kcp * 2:(kcp + 1) * 2, :, :].rearrange("kc p d -> p kc d"))
                S.add("sp", f, reads=("blob_out",), writes=("sbout:p%d" % (kcp % 2),),
                      dma="dout", tag="ldout%d.%d" % (i, kcp))

        def ln_stats(i):
            par = i % 2
            xk = "x:p%d" % par
            xoff = par * 2064
            if i == NCH - 1:
                def f(eng, xoff=xoff):
                    return eng.memset(apv(sb["x_f"], xoff + 256, [[258, 8], [1, 2]]), 0.0)
                S.add("dve", f, reads=(), writes=(xk,), tag="xpad%d" % i)
            def fxb(eng, xoff=xoff):
                return eng.tensor_copy(sb["scrb1"][:, 0:2064], apv(sb["x_f"], xoff, [[1, 2064]]))
            S.add("dve", fxb, reads=(xk,), writes=("scrb1:p0", "scrb1:p1"), tag="xb%d" % i)
            def fxq(eng, xoff=xoff):
                return eng.activation(sb["scrb2"][:, 0:2064], apv(sb["x_f"], xoff, [[1, 2064]]),
                                      AF.Square)
            S.add("act", fxq, reads=(xk,), writes=("scrb2:p0", "scrb2:p1"), tag="xsq%d" % i)
            for g in range(GD):
                mm(ps["P0"][0:1, 0:258], sb["onescol"][:, :],
                   sb["scrb1"][:, g * 258:(g + 1) * 258], "P0", g == 0, g == 7,
                   ["scrb1:p0", "scrb1:p1", "onescol"], tag="lnsum%d" % i)
            def fm(eng):
                return eng.tensor_scalar(sb["strow"][0:1, 516:774], ps["P0"][0:1, 0:258],
                                         1.0 / 1024.0, None, ALU.mult)
            S.add("dve", fm, reads=("P0",), writes=("strow_m",), tag="lnm%d" % i)
            for g in range(GD):
                mm(ps["P1"][0:1, 0:258], sb["onescol"][:, :],
                   sb["scrb2"][:, g * 258:(g + 1) * 258], "P1", g == 0, g == 7,
                   ["scrb2:p0", "scrb2:p1", "onescol"], tag="lnssq%d" % i)
            def fv1(eng):
                return eng.tensor_scalar(sb["strow"][0:1, 774:1032], ps["P1"][0:1, 0:258],
                                         1.0 / 1024.0, None, ALU.mult)
            S.add("dve", fv1, reads=("P1",), writes=("strow_v",), tag="lnv1%d" % i)
            def fmsq(eng):
                return eng.tensor_tensor(sb["strow"][0:1, 0:258], sb["strow"][0:1, 516:774],
                                         sb["strow"][0:1, 516:774], ALU.mult)
            S.add("dve", fmsq, reads=("strow_m",), writes=("strow_a",), tag="lnmsq%d" % i)
            def fv2(eng):
                return eng.tensor_tensor(sb["strow"][0:1, 774:1032], sb["strow"][0:1, 774:1032],
                                         sb["strow"][0:1, 0:258], ALU.subtract)
            S.add("dve", fv2, reads=("strow_v", "strow_a"), writes=("strow_v",), tag="lnv2%d" % i)
            def fln(eng):
                return eng.activation(sb["strow"][0:1, 774:1032], sb["strow"][0:1, 774:1032],
                                      AF.Ln, bias=1e-5)
            S.add("act", fln, reads=("strow_v", "cdb"), writes=("strow_v",), tag="lnln%d" % i)
            def fex(eng):
                return eng.activation(sb["strow"][0:1, 0:258], sb["strow"][0:1, 774:1032],
                                      AF.Exp, scale=-0.5)
            S.add("act", fex, reads=("strow_v", "cdb"), writes=("strow_a",), tag="lnex%d" % i)
            def fb1(eng):
                return eng.tensor_tensor(sb["strow"][0:1, 258:516], sb["strow"][0:1, 0:258],
                                         sb["strow"][0:1, 516:774], ALU.mult)
            S.add("dve", fb1, reads=("strow_a", "strow_m"), writes=("strow_b",), tag="lnb1%d" % i)
            def fb2(eng):
                return eng.tensor_scalar(sb["strow"][0:1, 258:516], sb["strow"][0:1, 258:516],
                                         -1.0, None, ALU.mult)
            S.add("dve", fb2, reads=("strow_b",), writes=("strow_b",), tag="lnb2%d" % i)
            mm(ps["P0"][:, 0:258], sb["ones1"][0:1, :], sb["strow"][0:1, 0:258],
               "P0", True, True, ["ones1", "strow_a"], tag="lnbca%d" % i)
            def fea(eng):
                return eng.activation(sb["lnab"][:, 0:258], ps["P0"][:, 0:258], AF.Copy)
            S.add("act", fea, reads=("P0",), writes=("lnab_a",), tag="lnea%d" % i)
            mm(ps["P1"][:, 0:258], sb["ones1"][0:1, :], sb["strow"][0:1, 258:516],
               "P1", True, True, ["ones1", "strow_b"], tag="lnbcb%d" % i)
            def feb(eng):
                return eng.activation(sb["lnab"][:, 258:516], ps["P1"][:, 0:258], AF.Copy)
            S.add("act", feb, reads=("P1",), writes=("lnab_b",), tag="lneb%d" % i)
        def ln_xn(i):
            par = i % 2
            xk = "x:p%d" % par
            xoff = par * 2064
            a_bc = apv(sb["lnab"], 0, [[0, 8], [1, 258]])
            b_bc = apv(sb["lnab"], 258, [[0, 8], [1, 258]])
            skeys = tuple("s:%d" % j for j in range(8))
            def fx1(eng, xoff=xoff, a_bc=a_bc):
                return eng.tensor_tensor(apv(sb["s_f"], 0, [[258, 8], [1, 258]]),
                                         apv(sb["x_f"], xoff, [[258, 8], [1, 258]]),
                                         a_bc, ALU.mult)
            S.add("dve", fx1, reads=(xk, "lnab_a"), writes=skeys, tag="xn1_%d" % i)
            def fx2(eng, b_bc=b_bc):
                return eng.tensor_tensor(apv(sb["xn_b"], 0, [[258, 8], [1, 258]]),
                                         apv(sb["s_f"], 0, [[258, 8], [1, 258]]),
                                         b_bc, ALU.add)
            S.add("dve", fx2, reads=skeys + ("lnab_b",), writes=("xn",), tag="xn2_%d" % i)

        def out_proj_phase_a(i):
            for kcp in (0, 1, 2, 4, 5, 6, 7):
                par = kcp % 2
                def fld(eng, kcp=kcp, par=par):
                    dst = sb["sb_out"][:, par * 2048:(par + 1) * 2048].rearrange(
                        "p (kc d) -> p kc d", kc=2)
                    return eng.dma_start(out=dst, in_=blobs["blob_out"][kcp * 2:(kcp + 1) * 2, :, :].rearrange("kc p d -> p kc d"))
                S.add("sp", fld, reads=("blob_out",), writes=("sbout:p%d" % par,),
                      dma="dout" if kcp % 2 == 0 else "dout1", tag="ldout%d.%d" % (i, kcp))
                for kci in range(2):
                    kc = kcp * 2 + kci
                    for j in range(GD):
                        bank = "P%d" % (4 + j // 2)
                        half = (j % 2) * 256
                        lhsT = sb["sb_out"][:, par * 2048 + kci * 1024 + j * 128:
                                            par * 2048 + kci * 1024 + (j + 1) * 128]
                        if kc < 8:
                            rhs = sb["y_b"][:, kc * 256:(kc + 1) * 256]
                            rk = "y"
                        else:
                            rhs = sb["zc"][:, (kc - 8) * 256:(kc - 7) * 256]
                            rk = "zc"
                        mm(ps[bank][:, half:half + 256], lhsT, rhs, bank,
                           (j % 2 == 0) and kc == 0, False,
                           ["sbout:p%d" % par, rk], tag="op%d.%d.%d" % (i, j, kc))
        def out_proj_a_evac(i, js):
            for j in js:
                bank = "P%d" % (4 + j // 2)
                half = (j % 2) * 256
                def fe(eng, j=j, bank=bank, half=half):
                    return eng.activation(sb["s_f"][:, j * 256:(j + 1) * 256],
                                          ps[bank][:, half:half + 256], AF.Copy)
                S.add("act", fe, reads=(bank,), writes=("s:%d" % j,), tag="ope%d.%d" % (i, j))

        def out_proj_phase_b(i):
            for kcp in (3,):
                par = kcp % 2
                def fld(eng, kcp=kcp, par=par):
                    dst = sb["sb_out"][:, par * 2048:(par + 1) * 2048].rearrange(
                        "p (kc d) -> p kc d", kc=2)
                    return eng.dma_start(out=dst, in_=blobs["blob_out"][kcp * 2:(kcp + 1) * 2, :, :].rearrange("kc p d -> p kc d"))
                S.add("sp", fld, reads=("blob_out",), writes=("sbout:p%d" % par,),
                      dma="dout" if kcp % 2 == 0 else "dout1", tag="ldoutb%d.%d" % (i, kcp))
                for kci in range(2):
                    kc = kcp * 2 + kci
                    for j in range(GD):
                        bank = "P%d" % (4 + j // 2)
                        half = (j % 2) * 256
                        lhsT = sb["sb_out"][:, par * 2048 + kci * 1024 + j * 128:
                                            par * 2048 + kci * 1024 + (j + 1) * 128]
                        rhs = sb["y_b"][:, kc * 256:(kc + 1) * 256]
                        mm(ps[bank][:, half:half + 256], lhsT, rhs, bank,
                           (j % 2 == 0) and kci == 0, False,
                           ["sbout:p%d" % par, "y"], tag="opb%d.%d.%d" % (i, j, kc))
            for q in range(4):
                bank = "P%d" % (4 + q)
                def fa(eng, q=q, bank=bank):
                    v = sb["s_f"][:, q * 512:(q + 1) * 512]
                    return eng.tensor_tensor(v, v, ps[bank][:, 0:512], ALU.add)
                S.add("dve", fa, reads=(bank, "s:%d" % (2 * q), "s:%d" % (2 * q + 1)),
                      writes=("s:%d" % (2 * q), "s:%d" % (2 * q + 1)), tag="opa%d.%d" % (i, q))

        def combine_phase(i):
            par = i % 2
            def f1(eng):
                return eng.tensor_tensor(sb["g_f"][:, :], sb["s_f"][:, 0:2048],
                                         sb["g_f"][:, :], ALU.mult)
            S.add("dve", f1, reads=tuple(["s:%d" % j for j in range(8)]) + ("g",),
                  writes=("g",), tag="cmb1_%d" % i)
            def f2(eng, par=par):
                return eng.tensor_tensor(sb["s_f"][:, 0:2048], sb["g_f"][:, :],
                                         apv(sb["x_f"], par * 2064, [[258, 8], [1, 256]]),
                                         ALU.add)
            S.add("dve", f2, reads=("g", "x:p%d" % par) + tuple(["s:%d" % j for j in range(8)]),
                  writes=tuple(["s:%d" % j for j in range(8)]), tag="cmb2_%d" % i)

        def store_phase(i):
            for tb in range(2):
                for gq in range(2):
                    bank = "P6" if gq == 0 else "P7"
                    for gg in range(4):
                        g = gq * 4 + gg
                        def ft(eng, g=g, gg=gg, tb=tb, bank=bank):
                            return eng.matmul(
                                bass.AP(ps[bank][:, :].tensor, gg * 128, [list(ps[bank][:, :].ap[0]), [1, 128]]),
                                sb["s_f"][:, g * 256 + tb * 128: g * 256 + (tb + 1) * 128],
                                sb["ident_f"][:, :], start=(gg == 0), stop=(gg == 3),
                                is_transpose=True, skip_group_check=True)
                        S.add("pe", ft, reads=["s:%d" % g, "ident_f"], writes=(bank,),
                              tag="tr%d.%d.%d" % (i, tb, g))
                    def fe(eng, gq=gq, tb=tb, bank=bank):
                        return eng.activation(sb["stage"][:, gq * 512:(gq + 1) * 512],
                                              ps[bank][:, :], AF.Copy)
                    S.add("act", fe, reads=(bank,), writes=("stage:0",),
                          tag="tre%d.%d.%d" % (i, tb, gq))
                def fs(eng, tb=tb, i=i):
                    return eng.dma_start(
                        out=ap["out"][i * T + tb * 128: i * T + (tb + 1) * 128, :],
                        in_=sb["stage"][:, 0:1024])
                S.add("sp", fs, reads=("stage:0",), writes=(), dma="dst",
                      tag="st%d.%d" % (i, tb))

        def in_proj_phase(i, jh):
            if jh == 0:
                def fc(eng):
                    dst = apv(sb["xzh"], 0, [[259, 16], [1, 3]])
                    return eng.tensor_copy(dst, sb["xzc"][:, :].rearrange("p (b c) -> p b c", b=16))
                S.add("dve", fc, reads=("xzc",), writes=("xzh",), tag="xzc%d" % i)
            for kc in range(8):
                par = (jh * 8 + kc) % 4
                def fld(eng, kc=kc, jh=jh, par=par):
                    dst = sb["sb_in"][:, par * 1024:(par + 1) * 1024]
                    return eng.dma_start(out=dst, in_=blobs["blob_in"][kc, :, jh * 1024:(jh + 1) * 1024])
                S.add("sp", fld, reads=("blob_in",), writes=("sbin:p%d" % par,),
                      dma="din" if kc % 2 == 0 else "din1",
                      tag="ldin%d.%d.%d" % (i, kc, jh))
                for j in range(8):
                    jg = jh * 8 + j
                    bank = "P%d" % (j // 2)
                    quarter = (j % 2) * 256
                    lhsT = sb["sb_in"][:, par * 1024 + j * 128: par * 1024 + (j + 1) * 128]
                    rhs = apv(sb["xn_b"], kc * 258 + 2, [[1, 256]])
                    mm(ps[bank][:, quarter:quarter + 256], lhsT, rhs, bank,
                       kc == 0 and (j % 2 == 0), False,
                       ["sbin:p%d" % par, "xn"], tag="ip%d.%d.%d.%d" % (i, jh, kc, j))
                if i == 0:
                    for j in range(8):
                        jg = jh * 8 + j
                        lhsT = sb["sb_in"][:, par * 1024 + j * 128: par * 1024 + (j + 1) * 128]
                        rhs = apv(sb["xn_b"], kc * 258, [[1, 2]])
                        mm(ps["P7"][:, jg * 2:(jg + 1) * 2], lhsT, rhs, "P7",
                           kc == 0 and jg == 0 and jh == 0, False,
                           ["sbin:p%d" % par, "xn"], tag="iph%d.%d" % (kc, jg))
            for b in range(4):
                j0 = jh * 8 + b * 2
                def fe(eng, b=b, j0=j0):
                    dst = apv(sb["xzh"], j0 * 259 + 3, [[259, 2], [1, 256]])
                    return eng.activation(dst, ps["P%d" % b][:, :].rearrange(
                        "p (b2 t) -> p b2 t", b2=2), AF.Copy)
                S.add("act", fe, reads=("P%d" % b,), writes=("xzh",),
                      tag="ipe%d.%d.%d" % (i, jh, b))
            if i == 0:
                def feh(eng, jh=jh):
                    dst = apv(sb["xzh"], jh * 8 * 259 + 1, [[259, 8], [1, 2]])
                    return eng.activation(dst, ps["P7"][:, jh * 16:jh * 16 + 16].rearrange(
                        "p (b2 t) -> p b2 t", b2=8), AF.Copy)
                S.add("act", feh, reads=("P7",), writes=("xzh",), tag="ipeh%d" % jh)
            if i == NCH - 1:
                def fz(eng, jh=jh):
                    return eng.memset(apv(sb["xzh"], jh * 8 * 259 + 257, [[259, 8], [1, 2]]), 0.0)
                S.add("dve", fz, reads=("xzh",), writes=("xzh",), tag="xzz%d.%d" % (i, jh))

        def conv_phase(i, br0):
            for br in (br0,):
                def ld(k):
                    par = (br * 4 + k) % 2
                    def fld(eng, br=br, k=k, par=par):
                        dst = sb["sb_cd"][:, par * 1024:(par + 1) * 1024]
                        return eng.dma_start(out=dst, in_=blobs["blob_cd"][br, k, :, :])
                    S.add("sp", fld, reads=("blob_cd",), writes=("sbcd:p%d" % par,),
                          dma="dcd" if k % 2 == 0 else "dcd1", tag="ldcd%d.%d.%d" % (i, br, k))
                ld(0)
                ld(1)
                for k in range(4):
                    par = (br * 4 + k) % 2
                    for g in range(GD):
                        bank = "P%d" % (g // 2)
                        half = (g % 2) * 256
                        blk = br * 8 + g
                        lhsT = sb["sb_cd"][:, par * 1024 + g * 128: par * 1024 + (g + 1) * 128]
                        rhs = apv(sb["xzh"], blk * 259 + k, [[1, 256]])
                        mm(ps[bank][:, half:half + 256], lhsT, rhs, bank,
                           k == 0 and (g % 2 == 0), False,
                           ["sbcd:p%d" % par, "xzh"], tag="cv%d.%d.%d.%d" % (i, br, k, g))
                    if k + 2 < 4:
                        ld(k + 2)
                for b in range(4):
                    if br == 0:
                        if b < 2:
                            dst = sb["xc"][:, b * 512:(b + 1) * 512]
                            wkey = "xc"
                        else:
                            par = i % 2
                            dst = sb["xco"][:, par * 1024 + (b - 2) * 512:
                                            par * 1024 + (b - 1) * 512]
                            wkey = "xco:p%d" % par
                    else:
                        dst = sb["zc"][:, b * 512:(b + 1) * 512]
                        wkey = "zc"
                    def fe(eng, b=b, dst=dst):
                        return eng.activation(dst, ps["P%d" % b][:, :], AF.Silu)
                    S.add("act", fe, reads=("P%d" % b,), writes=(wkey,),
                          tag="cve%d.%d.%d" % (i, br, b))

        def xz_tail_save(i, br0):
            def fts(eng, br0=br0):
                srcv = apv(sb["xzh"], br0 * 8 * 259 + 256, [[259, 8], [1, 3]])
                return eng.tensor_copy(
                    bass.AP(sb["xzc"][:, :].tensor, br0 * 24, [list(sb["xzc"][:, :].ap[0]), [3, 8], [1, 3]]),
                    srcv)
            S.add("dve", fts, reads=("xzh",), writes=("xzc",), tag="xzts%d.%d" % (i, br0))

        def xproj_phase(i):
            for kc in range(8):
                lhsT = sb["xpw"][:, kc * 96:(kc + 1) * 96]
                if kc < 4:
                    rhs = sb["xc"][:, kc * 256:(kc + 1) * 256]
                    rkey = "xc"
                else:
                    par = i % 2
                    rhs = sb["xco"][:, par * 1024 + (kc - 4) * 256:
                                    par * 1024 + (kc - 3) * 256]
                    rkey = "xco:p%d" % par
                mm(ps["P4"][0:96, 0:256], lhsT, rhs, "P4", kc == 0, kc == 7,
                   ["xpw", rkey], tag="xp%d.%d" % (i, kc))
            def f1(eng):
                return eng.activation(sb["dtr"][0:64, :], ps["P4"][0:64, 0:256], AF.Copy)
            S.add("act", f1, reads=("P4",), writes=("dtr",), tag="xpe1_%d" % i)
            def f2(eng):
                return eng.activation(sb["stage"][0:32, 768:1024], ps["P4"][64:96, 0:256], AF.Copy)
            S.add("act", f2, reads=("P4", "stage:0"), writes=("stage:0",), tag="xpe2_%d" % i)
            def f3(eng):
                return eng.activation(sb["bcB"][0:16, :], sb["stage"][0:16, 768:1024], AF.Copy)
            S.add("act", f3, reads=("stage:0",), writes=("bcB_B",), tag="xpe3_%d" % i)
            def f4(eng):
                return eng.dma_start(out=sb["bcB"][32:48, :], in_=sb["stage"][16:32, 768:1024])
            S.add("gp", f4, reads=("stage:0",), writes=("bcB_C",), dma="dbc", tag="xpe4_%d" % i)

        def dt_phase(i, js=None):
            for j in (range(GD) if js is None else js):
                bank = ("P4", "P5", "P6")[j % 3]
                half = 0
                lhsT = sb["dtw"][0:64, j * 128:(j + 1) * 128]
                mm(ps[bank][0:128, 0:256], lhsT, sb["dtr"][0:64, :], bank,
                   True, True, ["dtw", "dtr"], tag="dt%d.%d" % (i, j))
                scr = sb["stage"][:, (j % 4) * 256:(j % 4 + 1) * 256]
                def f1(eng, j=j, bank=bank, scr=scr):
                    return eng.activation(scr, ps[bank][:, 0:256], AF.Abs,
                                          bias=sb["dtb"][:, j:j + 1])
                S.add("act", f1, reads=(bank, "dtb", "stage:0"), writes=("stage:0",),
                      tag="dte1_%d.%d" % (i, j))
                def f4(eng, j=j, bank=bank):
                    return eng.activation(sb["delta"][:, j * 256:(j + 1) * 256],
                                          ps[bank][:, 0:256], AF.Relu,
                                          bias=sb["dtb"][:, j:j + 1])
                S.add("act", f4, reads=(bank, "dtb"), writes=("delta:%d" % j,), tag="dte4_%d.%d" % (i, j))
                def f2(eng, j=j, scr=scr):
                    return eng.activation(scr, scr, AF.Exp, scale=-1.0)
                S.add("act", f2, reads=("stage:0", "cdb"), writes=("stage:0",), tag="dte2_%d.%d" % (i, j))
                def f3(eng, j=j, scr=scr):
                    return eng.activation(scr, scr, AF.Ln, bias=1.0)
                S.add("act", f3, reads=("stage:0", "cdb"), writes=("stage:0",), tag="dte3_%d.%d" % (i, j))
                def f5(eng, j=j, scr=scr):
                    d = sb["delta"][:, j * 256:(j + 1) * 256]
                    return eng.tensor_tensor(d, d, scr, ALU.add)
                S.add("gp", f5, reads=("stage:0", "delta:%d" % j), writes=("delta:%d" % j,),
                      tag="dte5_%d.%d" % (i, j))
                if j < 4:
                    xsrc = sb["xc"][:, j * 256:(j + 1) * 256]
                    xkey = "xc"
                else:
                    par = i % 2
                    xsrc = sb["xco"][:, par * 1024 + (j - 4) * 256:
                                     par * 1024 + (j - 3) * 256]
                    xkey = "xco:p%d" % par
                def fu(eng, j=j, xsrc=xsrc):
                    return eng.tensor_tensor(sb["u_b"][:, j * 256:(j + 1) * 256],
                                             sb["delta"][:, j * 256:(j + 1) * 256],
                                             xsrc, ALU.mult)
                S.add("gp", fu, reads=("delta:%d" % j, xkey), writes=("u:%d" % j,), tag="u%d.%d" % (i, j))

        def gate_phase(i):
            for kc in range(8):
                lhsT = sb["gw"][:, kc * 8:(kc + 1) * 8]
                rhs = sb["xn_b"][:, kc * 258:kc * 258 + 256]
                mm(ps["P6"][0:8, 0:256], lhsT, rhs, "P6", kc == 0, kc == 7,
                   ["gw", "xn"], tag="gl%d.%d" % (i, kc))
            def ft(eng):
                return eng.activation(sb["gl_f"][0:8, :], ps["P6"][0:8, 0:256], AF.Tanh,
                                      scale=0.5)
            S.add("act", ft, reads=("P6", "cdb"), writes=("gl",), tag="gt%d" % i)
            for tb in range(2):
                def f(eng, tb=tb):
                    return eng.matmul(
                        bass.AP(ps["P6"][:, :].tensor, 256 + tb * 8, [list(ps["P6"][:, :].ap[0]), [1, 8]]),
                        sb["gl_f"][0:8, tb * 128:(tb + 1) * 128], sb["ident_f"][0:8, 0:8],
                        start=(tb == 0), stop=(tb == 1), is_transpose=True,
                        skip_group_check=True)
                S.add("pe", f, reads=("gl", "ident_f"), writes=("P6",), tag="gtr%d.%d" % (i, tb))
            def fs(eng):
                return eng.tensor_scalar(sb["st_f"][:, :], ps["P6"][:, 256:272],
                                         sb["rsh"][:, :], sb["rsh"][:, :], ALU.mult, ALU.add)
            S.add("dve", fs, reads=("P6", "rsh"), writes=("st",), tag="gs%d" % i)
            for tb in range(2):
                def f8(eng, tb=tb):
                    return eng.max(sb["m8"][:, tb * 8:(tb + 1) * 8], sb["st_f"][:, tb * 8:(tb + 1) * 8])
                S.add("dve", f8, reads=("st",), writes=("m8",), tag="gm8_%d.%d" % (i, tb))
            def fd(eng):
                return eng.tensor_tensor(sb["den"][:, :], apv(sb["m8"], 0, [[8, 2]]),
                                         apv(sb["m8"], 1, [[8, 2]]), ALU.add)
            S.add("dve", fd, reads=("m8",), writes=("den",), tag="gden%d" % i)
            def fr(eng):
                return eng.reciprocal(sb["rden"][:, :], sb["den"][:, :])
            S.add("dve", fr, reads=("den",), writes=("rden",), tag="grec%d" % i)
            for tb in range(2):
                def fi(eng, tb=tb):
                    return eng.tensor_scalar(sb["ind"][:, tb * 8:(tb + 1) * 8],
                                             sb["st_f"][:, tb * 8:(tb + 1) * 8],
                                             sb["m8"][:, tb * 8 + 1:tb * 8 + 2], None,
                                             ALU.is_ge)
                S.add("dve", fi, reads=("st", "m8"), writes=("ind",), tag="gind%d.%d" % (i, tb))
                def fg1(eng, tb=tb):
                    return eng.tensor_scalar(sb["gt"][:, tb * 8:(tb + 1) * 8],
                                             sb["st_f"][:, tb * 8:(tb + 1) * 8],
                                             sb["rden"][:, tb:tb + 1], None, ALU.mult)
                S.add("dve", fg1, reads=("st", "rden"), writes=("gt",), tag="gg1_%d.%d" % (i, tb))
            def fg2(eng):
                return eng.tensor_tensor(sb["gt"][:, :], sb["gt"][:, :], sb["ind"][:, :],
                                         ALU.mult)
            S.add("dve", fg2, reads=("gt", "ind"), writes=("gt",), tag="gg2_%d" % i)
            for tb in range(2):
                def fb(eng, tb=tb):
                    return eng.matmul(
                        ps["P6"][0:8, tb * 128:(tb + 1) * 128],
                        sb["gt"][:, tb * 8:(tb + 1) * 8], sb["ident_f"][:, :],
                        start=(tb == 0), stop=(tb == 1), is_transpose=True,
                        skip_group_check=True)
                S.add("pe", fb, reads=("gt", "ident_f"), writes=("P6",), tag="gbk%d.%d" % (i, tb))
            def fe(eng):
                return eng.activation(sb["gcm"][0:8, :], ps["P6"][0:8, 0:256], AF.Copy)
            S.add("act", fe, reads=("P6",), writes=("gcm",), tag="gev%d" % i)


        def gate_bcast(i):
            def fgd(eng):
                return eng.dma_start(out=blobs["ge_dram"][0:8, :], in_=sb["gcm"][0:8, :])
            S.add("gp", fgd, reads=("gcm",), writes=("ged",), dma="dge0", tag="ged%d" % i)
            def fgb(eng):
                srcv = bass.AP(blobs["ge_dram"].tensor, 0, [[0, 128], [256, 8], [1, 256]])
                return eng.dma_start(out=sb["ge_all"][:, :].rearrange("p (n t) -> p n t", n=8),
                                     in_=srcv)
            S.add("gp", fgb, reads=("ged",), writes=("ge_all",), dma="dge1", tag="geb%d" % i)

        def bbcb_phase(i):
            def f0(eng):
                return eng.dma_start(out=blobs["bc_dram"][0:16, :], in_=sb["bcB"][0:16, :])
            S.add("gp", f0, reads=("bcB_B",), writes=("bcd",),
                  dma="dbb0", tag="bcd%d" % i)
            def f0c(eng):
                return eng.dma_start(out=blobs["bc_dram"][32:48, :], in_=sb["bcB"][32:48, :])
            S.add("gp", f0c, reads=("bcB_C",), writes=("bcd",),
                  dma="dbb3", tag="bcdc%d" % i)
            for which, base, wk in [(0, 0, "Bb"), (1, 32, "Cb")]:
                def fd(eng, base=base, wk=wk):
                    srcv = bass.AP(blobs["bc_dram"].tensor, base * 256,
                                   [[0, 128], [256, 16], [1, 256]])
                    return eng.dma_start(out=sb[wk][:, :].rearrange("p (n t) -> p n t", n=16),
                                         in_=srcv)
                S.add("gp", fd, reads=("bcd",), writes=(wk,),
                      dma="dbb%d" % (1 + which), tag="bc%d.%d" % (i, which))

        def scan_group(i, g):
            dpar = g % 2
            dko = dpar * 4096
            dk = "dBx:p%d" % dpar
            def fb(eng, g=g, dko=dko):
                u_bc = apv(sb["u_b"], g * 256, [[0, 16], [1, 256]])
                return eng.tensor_tensor(apv(sb["dBx"], dko, [[256, 16], [1, 256]]),
                                         u_bc, apv(sb["Bb"], 0, [[256, 16], [1, 256]]),
                                         ALU.mult)
            S.add("dve", fb, reads=("u:%d" % g, "Bb", dk), writes=(dk,), tag="dbx%d.%d" % (i, g))
            for n in range(DSTATE):
                slot = (g * DSTATE + n) % 8
                def fx(eng, g=g, n=n, slot=slot):
                    return eng.activation(sb["dA"][:, slot * 256:(slot + 1) * 256],
                                          sb["delta"][:, g * 256:(g + 1) * 256],
                                          AF.Exp, scale=sb["A_neg"][:, g * 16 + n:g * 16 + n + 1])
                S.add("act", fx, reads=("delta:%d" % g, "A_neg", "dA:s%d" % slot),
                      writes=("dA:s%d" % slot,), tag="ex%d.%d.%d" % (i, g, n))
                if n % 4 == 3:
                    q0 = n - 3
                    s0 = (g * DSTATE + q0) % 8
                    skeys = tuple("dA:s%d" % ((s0 + k) % 8) for k in range(4))
                    bkeys = skeys[1:]
                    if i > 0:
                        def fm(eng, g=g, q0=q0, s0=s0):
                            return eng.tensor_tensor(
                                sb["fixt"][:, 0:3],
                                apv(sb["dA"], (s0 + 1) * 256, [[256, 3], [1, 1]]),
                                apv(sb["carry"], g * 16 + q0 + 1, [[1, 3], [1, 1]]),
                                ALU.mult)
                        S.add("dve", fm, reads=bkeys + ("carry:g%d" % g, "fixt"),
                              writes=("fixt",), tag="sfm%d.%d.%d" % (i, g, q0))
                        def fa(eng, g=g, q0=q0, dko=dko):
                            v = apv(sb["dBx"], dko + (q0 + 1) * 256, [[256, 3], [1, 1]])
                            return eng.tensor_tensor(v, v, sb["fixt"][:, 0:3], ALU.add)
                        S.add("dve", fa, reads=("fixt", dk), writes=(dk,),
                              tag="sfa%d.%d.%d" % (i, g, q0))
                    def fz(eng, s0=s0):
                        return eng.memset(apv(sb["dA"], (s0 + 1) * 256, [[256, 3], [1, 1]]), 0.0)
                    S.add("dve", fz, reads=bkeys, writes=bkeys, tag="sfz%d.%d.%d" % (i, g, q0))
                    def fsc(eng, g=g, q0=q0, s0=s0, i=i, dko=dko):
                        init = 0.0 if i == 0 else sb["carry"][:, g * 16 + q0:g * 16 + q0 + 1]
                        return eng.tensor_tensor_scan(
                            sb["h_b"][:, q0 * 256:(q0 + 4) * 256],
                            apv(sb["dA"], s0 * 256, [[1, 1024]]),
                            apv(sb["dBx"], dko + q0 * 256, [[1, 1024]]),
                            init, ALU.mult, ALU.add)
                    S.add("dve", fsc, reads=skeys + (dk, "carry:g%d" % g, "h"),
                          writes=("h",) + skeys, tag="sc%d.%d.%d" % (i, g, q0))
            def fcs(eng, g=g):
                src = apv(sb["h_b"], 255, [[256, 16]])
                return eng.tensor_copy(sb["carry"][:, g * 16:(g + 1) * 16], src)
            S.add("dve", fcs, reads=("h",), writes=("carry:g%d" % g,), tag="cs%d.%d" % (i, g))
            def fhc(eng, dko=dko):
                return eng.tensor_tensor(apv(sb["dBx"], dko, [[256, 16], [1, 256]]),
                                         apv(sb["h_b"], 0, [[256, 16], [1, 256]]),
                                         apv(sb["Cb"], 0, [[256, 16], [1, 256]]), ALU.mult)
            S.add("dve", fhc, reads=("h", "Cb", dk), writes=(dk,), tag="hc%d.%d" % (i, g))

        def idents_phase(i, g, first=False):
            dpar = g % 2
            dko = dpar * 4096
            dk = "dBx:p%d" % dpar
            half = dpar * 256
            for n in range(DSTATE):
                mm(ps["P7"][:, half:half + 256], sb["ident_b"][:, :],
                   apv(sb["dBx"], dko + n * 256, [[1, 256]]), "P7",
                   n == 0 and (dpar == 0 or first), False, [dk, "ident_b"],
                   tag="yid%d.%d.%d" % (i, g, n))
            if g < 4:
                xsrc = sb["xc"][:, g * 256:(g + 1) * 256]
                xkey = "xc"
            else:
                par = i % 2
                xsrc = sb["xco"][:, par * 1024 + (g - 4) * 256:
                                 par * 1024 + (g - 3) * 256]
                xkey = "xco:p%d" % par
            mm(ps["P7"][:, half:half + 256], sb["dd_b"][:, g * 128:(g + 1) * 128],
               xsrc, "P7", False, True,
               ["dd_b", xkey], tag="ydd%d.%d" % (i, g))
            def fy(eng, g=g, half=half):
                return eng.activation(sb["y_b"][:, g * 256:(g + 1) * 256],
                                      ps["P7"][:, half:half + 256], AF.Copy)
            S.add("act", fy, reads=("P7",), writes=("y",), tag="ye%d.%d" % (i, g))

        def ws1_phase(i):
            for kc in range(8):
                for hg in range(4):
                    bank = "P4" if hg < 2 else "P5"
                    half = (hg % 2) * 256
                    lhsT = sb["ws1_b"][:, kc * 512 + hg * 128: kc * 512 + (hg + 1) * 128]
                    rhs = sb["xn_b"][:, kc * 258:kc * 258 + 256]
                    mm(ps[bank][:, half:half + 256], lhsT, rhs, bank,
                       kc == 0 and hg % 2 == 0, False, ["ws1_b", "xn"],
                       tag="ws1_%d.%d.%d" % (i, kc, hg))
            for hg in range(4):
                bank = "P4" if hg < 2 else "P5"
                half = (hg % 2) * 256
                def fe(eng, hg=hg, bank=bank, half=half):
                    return eng.activation(sb["hsh"][:, hg * 256:(hg + 1) * 256],
                                          ps[bank][:, half:half + 256], AF.Gelu,
                                          bias=sb["bs1c"][:, hg:hg + 1])
                S.add("act", fe, reads=(bank, "bs1c"), writes=("hsh",), tag="ws1e%d.%d" % (i, hg))

        def ffn_expert(i, e):
            par = e % 2
            for kh in range(2):
                spar = (e * 2 + kh) % 2
                def fld1(eng, e=e, kh=kh, spar=spar):
                    dst = sb["sb_w1"][:, spar * 2048:(spar + 1) * 2048].rearrange(
                        "p (kc h) -> p kc h", kc=4)
                    return eng.dma_start(out=dst, in_=blobs["blob_w1"][e, kh * 4:(kh + 1) * 4, :, :].rearrange("kc p h -> p kc h"))
                S.add("sp", fld1, reads=("blob_w1",), writes=("sbw1:p%d" % spar,),
                      dma="dw1" if spar == 0 else "dw1b", tag="ldw1%d.%d.%d" % (i, e, kh))
                for kci in range(4):
                    kc = kh * 4 + kci
                    for hg in range(4):
                        bank = "P4" if hg < 2 else "P5"
                        half = (hg % 2) * 256
                        lhsT = sb["sb_w1"][:, spar * 2048 + kci * 512 + hg * 128:
                                           spar * 2048 + kci * 512 + (hg + 1) * 128]
                        rhs = sb["xn_b"][:, kc * 258:kc * 258 + 256]
                        mm(ps[bank][:, half:half + 256], lhsT, rhs, bank,
                           kc == 0 and hg % 2 == 0, False,
                           ["sbw1:p%d" % spar, "xn"], tag="w1_%d.%d.%d.%d" % (i, e, kc, hg))
            for hg in range(4):
                bank = "P4" if hg < 2 else "P5"
                half = (hg % 2) * 256
                def fe(eng, e=e, hg=hg, bank=bank, half=half, par=par):
                    return eng.activation(sb["scrb2"][:, par * 1024 + hg * 256:
                                                      par * 1024 + (hg + 1) * 256],
                                          ps[bank][:, half:half + 256], AF.Gelu,
                                          bias=sb["b1c"][:, e * 4 + hg:e * 4 + hg + 1])
                S.add("act", fe, reads=(bank, "b1c"), writes=("scrb2:p%d" % par,),
                      tag="gel%d.%d.%d" % (i, e, hg))
        def hgate_phase(i, e):
            par = e % 2
            def fhg(eng, e=e, par=par):
                geb = apv(sb["ge_all"], e * 256, [[0, 4], [1, 256]])
                return eng.tensor_tensor(
                    apv(sb["scrb1"], par * 1024, [[256, 4], [1, 256]]),
                    apv(sb["scrb2"], par * 1024, [[256, 4], [1, 256]]), geb, ALU.mult)
            S.add("dve", fhg, reads=("scrb2:p%d" % par, "ge_all", "scrb1:p%d" % par),
                  writes=("scrb1:p%d" % par,), tag="hg%d.%d" % (i, e))
        def ffn_w2(i, e):
            par = e % 2
            for kh in range(2):
                spar = (e * 2 + kh) % 2
                def fld2(eng, e=e, kh=kh, spar=spar):
                    dst = sb["sb_w2"][:, spar * 2048:(spar + 1) * 2048].rearrange(
                        "p (kc d) -> p kc d", kc=2)
                    return eng.dma_start(out=dst, in_=blobs["blob_w2"][e, kh * 2:(kh + 1) * 2, :, :].rearrange("kc p d -> p kc d"))
                S.add("sp", fld2, reads=("blob_w2",), writes=("sbw2:p%d" % spar,),
                      dma="dw2" if spar == 0 else "dw2b", tag="ldw2%d.%d.%d" % (i, e, kh))
                for kci in range(2):
                    kc = kh * 2 + kci
                    for j in range(8):
                        bank = "P%d" % (j // 2)
                        half = (j % 2) * 256
                        lhsT = sb["sb_w2"][:, spar * 2048 + kci * 1024 + j * 128:
                                           spar * 2048 + kci * 1024 + (j + 1) * 128]
                        rhs = apv(sb["scrb1"], par * 1024 + kc * 256, [[1, 256]])
                        mm(ps[bank][:, half:half + 256], lhsT, rhs, bank,
                           e == 0 and kc == 0 and (j % 2 == 0), False,
                           ["sbw2:p%d" % spar, "scrb1:p%d" % par],
                           tag="w2_%d.%d.%d.%d" % (i, e, kc, j))

        def ffn_tail(i):
            for j in range(8):
                bank = "P%d" % (j // 2)
                half = (j % 2) * 256
                mm(ps[bank][:, half:half + 256], sb["b2_b"][0:8, j * 128:(j + 1) * 128],
                   sb["gcm"][0:8, :], bank, False, False, ["b2_b", "gcm"],
                   tag="b2_%d.%d" % (i, j))
            for kc in range(4):
                for j in range(8):
                    bank = "P%d" % (j // 2)
                    half = (j % 2) * 256
                    lhsT = sb["ws2_b"][:, kc * 1024 + j * 128: kc * 1024 + (j + 1) * 128]
                    rhs = sb["hsh"][:, kc * 256:(kc + 1) * 256]
                    mm(ps[bank][:, half:half + 256], lhsT, rhs, bank,
                       False, kc == 3, ["ws2_b", "hsh"], tag="ws2_%d.%d.%d" % (i, kc, j))
            for j in range(8):
                bank = "P%d" % (j // 2)
                half = (j % 2) * 256
                def fe(eng, j=j, bank=bank, half=half):
                    return eng.activation(sb["g_f"][:, j * 256:(j + 1) * 256],
                                          ps[bank][:, half:half + 256], AF.Silu,
                                          bias=sb["bs2c"][:, j:j + 1])
                S.add("act", fe, reads=(bank, "bs2c"), writes=("g",), tag="ge%d.%d" % (i, j))

        def dbg_dump(name, tensor_key, sbname, cast=True):
            def f(eng, name=name, sbname=sbname):
                return eng.dma_start(out=ap[name], in_=sb[sbname][:, 0:ap[name].shape[1]])
            S.add("gp", f, reads=(tensor_key,), writes=(), dma="dbc", tag="dbg:" + name)

        GSPLIT = 6
        ln_stats(0)
        late_setup()
        for i in range(NCH + 1):
            if i < NCH:
                ln_xn(i)
            if i >= 1:
                scan_group(i - 1, GSPLIT)
            if i < NCH:
                in_proj_phase(i, 0)
            if i >= 1:
                scan_group(i - 1, GSPLIT + 1)
                out_proj_phase_a(i - 1)
                out_proj_a_evac(i - 1, (6, 7, 4, 5, 0, 1, 2, 3))
            if i < NCH:
                conv_phase(i, 0)
            if i < NCH:
                gate_phase(i)
                xproj_phase(i)
                bbcb_phase(i)
            if i >= 1:
                idents_phase(i - 1, GSPLIT + 1, first=True)
            if i < NCH:
                dt_phase(i)
            if i >= 1:
                idents_phase(i - 1, GSPLIT, first=True)
            if i < NCH:
                gate_bcast(i)
                scan_group(i, 0)
                in_proj_phase(i, 1)
                scan_group(i, 1)
                conv_phase(i, 1)
                if i == 0:
                    late_setup_b()
                if DBG_CHUNK == i:
                    dbg_dump("dbg_xz", "xzh", "xzh")
                    dbg_dump("dbg_bb", "Bb", "Bb")
                    dbg_dump("dbg_cb", "Cb", "Cb")
                    dbg_dump("dbg_delta", "delta", "delta")
                    dbg_dump("dbg_u", "u", "u_b")
            if i < NCH:
                for g in range(GD):
                    ffn_expert(i, g)
                    if g < GSPLIT:
                        idents_phase(i, g)
                    if g >= 1:
                        ffn_w2(i, g - 1)
                    hgate_phase(i, g)
                    if g + 2 < GSPLIT:
                        scan_group(i, g + 2)
                ffn_w2(i, GD - 1)
                ws1_phase(i)
            if i >= 1:
                out_proj_phase_b(i - 1)
                combine_phase(i - 1)
                store_phase(i - 1)
            if i < NCH:
                ffn_tail(i)
                xz_tail_save(i, 0)
                xz_tail_save(i, 1)
                if i + 1 < NCH:
                    prefetch(i + 1)
                    ln_stats(i + 1)
                if DBG_CHUNK == i:
                    dbg_dump("dbg_gf", "g", "g_f")
                    dbg_dump("dbg_hg", "scrb1:p1", "scrb1")
                    dbg_dump("dbg_gel", "scrb2:p1", "scrb2")
                    dbg_dump("dbg_ge", "ge_all", "ge_all")

        S.resolve()

        eng_of = {"sp": "sync", "pe": "tensor", "act": "scalar",
                  "dve": "vector", "gp": "gpsimd"}

        with nc.Block() as block:
            def section(engine_name):
                def run(eng):
                    for op in S.ops:
                        if op["e"] != engine_name:
                            continue
                        for (s, v) in op["waits"]:
                            eng.wait_ge(sems[s], v)
                        inst = op["fn"](eng)
                        if getattr(S, "record", None) is not None:
                            try:
                                S.record.append((op["tag"], inst.ins.name))
                            except Exception:
                                pass
                        if op["inc"] is not None:
                            s, step = op["inc"]
                            inst.then_inc(sems[s], step)
                    if engine_name == "sp":
                        eng.wait_ge(sems["dst"], S.final.get("dst", 0))
                return run

            block.sync(section("sp"))
            block.tensor(section("pe"))
            block.scalar(section("act"))
            block.vector(section("dve"))
            block.gpsimd(section("gp"))

    return nc


def host_prep(inputs, L):
    f = np.float32
    W1 = np.asarray(inputs["L0_ffn_W1"], f)
    W2 = np.asarray(inputs["L0_ffn_W2"], f)
    d = {}
    d["w_in"] = np.ascontiguousarray(np.asarray(inputs["L0_attn_in_proj_w"], f).T).reshape(8, 128, 2048)
    d["w_out"] = np.ascontiguousarray(np.asarray(inputs["L0_attn_out_proj_w"], f).T).reshape(16, 128, 1024)
    d["w_w1"] = np.ascontiguousarray(W1.reshape(8, 8, 128, 512))
    d["w_w2"] = np.ascontiguousarray(W2.reshape(8, 4, 128, 1024))
    d["w_ws1"] = np.ascontiguousarray(np.asarray(inputs["L0_ffn_Ws1"], f).reshape(8, 128, 512))
    d["w_ws2"] = np.ascontiguousarray(np.asarray(inputs["L0_ffn_Ws2"], f).reshape(4, 128, 1024))
    cd = np.zeros((2, 4, 128, 1024), f)
    for br, wname in enumerate(["L0_attn_conv_x_w", "L0_attn_conv_z_w"]):
        wk = np.asarray(inputs[wname], f)[:, 0, :]
        for k in range(4):
            for g in range(8):
                idx = np.arange(128)
                cd[br, k, idx, g * 128 + idx] = wk[g * 128 + idx, k]
    d["w_cd"] = cd
    d["aw_alog"] = np.ascontiguousarray(
        np.asarray(inputs["L0_attn_A_log"], f).reshape(8, 128, 16).transpose(1, 0, 2).reshape(128, 128))
    d["aw_dtb"] = np.ascontiguousarray(np.asarray(inputs["L0_attn_dt_proj_b"], f).reshape(8, 128).T)
    dd = np.zeros((128, 1024), f)
    Dv = np.asarray(inputs["L0_attn_D"], f)
    for g in range(8):
        idx = np.arange(128)
        dd[idx, g * 128 + idx] = Dv[g * 128 + idx]
    d["aw_dd"] = dd
    b1 = np.asarray(inputs["L0_ffn_b1"], f)
    d["aw_b1"] = np.ascontiguousarray(b1.reshape(8, 4, 128).transpose(2, 0, 1).reshape(128, 32))
    d["aw_bs1"] = np.ascontiguousarray(np.asarray(inputs["L0_ffn_bs1"], f).reshape(4, 128).T)
    d["aw_bs2"] = np.ascontiguousarray(np.asarray(inputs["L0_ffn_bs2"], f).reshape(8, 128).T)
    d["aw_rs"] = np.full((128, 1), np.asarray(inputs["L0_ffn_route_scale"], f))
    d["aw_xpw"] = np.ascontiguousarray(
        np.asarray(inputs["L0_attn_x_proj_w"], f).T.reshape(8, 128, 96).transpose(1, 0, 2).reshape(128, 768))
    d["aw_dtw"] = np.ascontiguousarray(np.asarray(inputs["L0_attn_dt_proj_w"], f).T)
    d["aw_gw"] = np.ascontiguousarray(
        np.asarray(inputs["L0_ffn_gate_W"], f).reshape(8, 128, 8).transpose(1, 0, 2).reshape(128, 64))
    d["aw_b2"] = np.ascontiguousarray(np.asarray(inputs["L0_ffn_b2"], f))
    d["cident"] = np.eye(128, dtype=f)
    d["cidentb"] = np.eye(128, dtype=ml_dtypes.bfloat16)
    d["cones1"] = np.ones((1, 128), f)
    d["conesc"] = np.ones((128, 1), ml_dtypes.bfloat16)
    return d


def kernel(**inputs):
    x = np.asarray(inputs["L0_attn_hidden_states"], np.float32)
    B, L, Dm = x.shape
    assert Dm == D
    shared = host_prep(inputs, L)
    nc = build_nc(L)
    in_maps = []
    for b in range(B):
        m = dict(shared)
        m["x_t"] = np.ascontiguousarray(x[b].T)
        in_maps.append(m)
    res = run_bass_kernel_spmd(nc, in_maps, core_ids=list(range(B)))
    out = np.stack([np.asarray(res.results[b]["out"]) for b in range(B)], axis=0)
    return out.astype(np.float32)


# version 65
# speedup vs baseline: 1.1546x; 1.0181x over previous
import numpy as np
import ml_dtypes
import concourse.bass as bass
import concourse.mybir as mybir
from concourse.bass_utils import run_bass_kernel_spmd

F32 = mybir.dt.float32
BF16 = mybir.dt.bfloat16
AF = mybir.ActivationFunctionType
ALU = mybir.AluOpType

D = 1024
GD = 8
DSTATE = 16
DTRANK = 64
E = 8
T = 256

DBG_CHUNK = None
RECORD = False
LAST_SCHED = None

ENGINES = ("sp", "pe", "act", "dve", "gp")
ENG_SEM = {e: "c_" + e for e in ENGINES}


class Sched:
    def __init__(self):
        self.ops = []
        self.dma_sems = set()

    def add(self, engine, fn, reads=(), writes=(), dma=None, tag=""):
        if dma is not None:
            self.dma_sems.add(dma)
        self.ops.append(dict(e=engine, fn=fn, r=tuple(reads), w=tuple(writes),
                             dma=dma, tag=tag, waits=None, inc=None))

    def resolve(self):
        last_w = {}
        readers = {}
        dma_last = {}
        edges = [[] for _ in self.ops]
        for i, op in enumerate(self.ops):
            deps = set()
            for k in op["r"]:
                if k in last_w:
                    deps.add(last_w[k])
            for k in op["w"]:
                if k in last_w:
                    deps.add(last_w[k])
                for r in readers.get(k, ()):
                    deps.add(r)
            if op["dma"] is not None and op["dma"] in dma_last:
                deps.add(dma_last[op["dma"]])
            deps.discard(i)
            edges[i] = sorted(deps)
            for k in op["r"]:
                readers.setdefault(k, []).append(i)
            for k in op["w"]:
                last_w[k] = i
                readers[k] = []
            if op["dma"] is not None:
                dma_last[op["dma"]] = i

        need_inc = [False] * len(self.ops)
        for i, op in enumerate(self.ops):
            for j in edges[i]:
                p = self.ops[j]
                if p["dma"] is None and op["dma"] is None and \
                        p["e"] == "pe" and op["e"] == "pe":
                    continue
                need_inc[j] = True
        for i, op in enumerate(self.ops):
            if op["dma"] is not None:
                need_inc[i] = True

        ev = [None] * len(self.ops)
        eng_cnt = {e: 0 for e in ENGINES}
        dma_cnt = {}
        for i, op in enumerate(self.ops):
            if not need_inc[i]:
                continue
            if op["dma"] is not None:
                s = op["dma"]
                dma_cnt[s] = dma_cnt.get(s, 0) + 1
                ev[i] = (s, dma_cnt[s] * 16)
                op["inc"] = (s, 16)
            else:
                s = ENG_SEM[op["e"]]
                eng_cnt[op["e"]] += 1
                ev[i] = (s, eng_cnt[op["e"]])
                op["inc"] = (s, 1)
        clocks = {e: {} for e in ENGINES}
        for i, op in enumerate(self.ops):
            e = op["e"]
            waits = []
            for j in edges[i]:
                p = self.ops[j]
                if p["dma"] is None and op["dma"] is None and \
                        p["e"] == "pe" and op["e"] == "pe":
                    continue
                sem, val = ev[j]
                if clocks[e].get(sem, 0) >= val:
                    continue
                waits.append((sem, val))
                clocks[e][sem] = val
            op["waits"] = waits
        self.final = {("c_" + e): c for e, c in eng_cnt.items()}
        self.final.update({s: c * 16 for s, c in dma_cnt.items()})


def build_nc(L):
    NCH = L // T
    nc = bass.Bass("TRN2", target_bir_lowering=False)
    ap = {}

    def din(name, shape, dt=F32):
        ap[name] = nc.dram_tensor(name, list(shape), dt, kind="ExternalInput").ap()

    din("x_t", (D, L))
    ap["out"] = nc.dram_tensor("out", [L, D], F32, kind="ExternalOutput").ap()
    if DBG_CHUNK is not None:
        for nm, shape in [("dbg_gf", (128, 2048)), ("dbg_hg", (128, 2048)),
                          ("dbg_gel", (128, 2048)), ("dbg_ge", (128, 512)),
                          ("dbg_xz", (128, 16 * 259)), ("dbg_bb", (128, 4096)),
                          ("dbg_cb", (128, 4096)), ("dbg_h", (128, 4096)),
                          ("dbg_da", (128, 2048)), ("dbg_delta", (128, 2048)),
                          ("dbg_u", (128, 2048))]:
            ap[nm] = nc.dram_tensor(nm, list(shape), F32, kind="ExternalOutput").ap()

    din("w_in", (8, 128, 2048))
    din("w_out", (16, 128, 1024))
    din("w_w1", (8, 8, 128, 512))
    din("w_w2", (8, 4, 128, 1024))
    din("w_ws1", (8, 128, 512))
    din("w_ws2", (4, 128, 1024))
    din("w_cd", (2, 4, 128, 1024))
    din("aw_alog", (128, 128))
    din("aw_dtb", (128, 8))
    din("aw_dd", (128, 1024))
    din("aw_b1", (128, 32))
    din("aw_bs1", (128, 4))
    din("aw_bs2", (128, 8))
    din("aw_rs", (128, 1))
    din("aw_xpw", (128, 768))
    din("aw_dtw", (64, 1024))
    din("aw_gw", (128, 64))
    din("aw_b2", (8, 1024))
    din("cident", (128, 128))
    din("cidentb", (128, 128), BF16)
    din("cones1", (1, 128))
    din("conesc", (128, 1), BF16)

    blobs = {}
    blobs["bc_dram"] = nc.dram_tensor("bc_dram", [48, 256], BF16, kind="Internal").ap()
    blobs["ge_dram"] = nc.dram_tensor("ge_dram", [8, 256], BF16, kind="Internal").ap()
    for nm, shape in [("blob_in", (8, 128, 2048)), ("blob_out", (16, 128, 1024)),
                      ("blob_w1", (8, 8, 128, 512)), ("blob_w2", (8, 4, 128, 1024)),
                      ("blob_ws1", (8, 128, 512)), ("blob_ws2", (4, 128, 1024)),
                      ("blob_cd", (2, 4, 128, 1024))]:
        blobs[nm] = nc.dram_tensor(nm, list(shape), BF16, kind="Internal").ap()

    from contextlib import ExitStack
    with ExitStack() as ctx:
        En = ctx.enter_context
        sb = {}

        def sbuf(name, shape, dt):
            sb[name] = En(nc.sbuf_tensor(name, list(shape), dt))

        sbuf("x_f", (128, 2 * 8 * 258), F32)
        sbuf("xn_b", (128, 8 * 258), BF16)
        sbuf("scrb1", (128, 2080), BF16)
        sbuf("scrb2", (128, 2080), BF16)
        sbuf("lnab", (128, 516), F32)
        sbuf("strow", (1, 1032), F32)
        sbuf("xzh", (128, 16 * 259), BF16)
        sbuf("xc", (128, 1024), BF16)
        sbuf("xco", (128, 2048), BF16)
        sbuf("zc", (128, 2048), BF16)
        sbuf("dtr", (64, 256), BF16)
        sbuf("bcB", (48, 256), BF16)
        sbuf("delta", (128, 2080), F32)
        sbuf("u_b", (128, 2048), BF16)
        sbuf("Bb", (128, 4096), BF16)
        sbuf("Cb", (128, 4096), BF16)
        sbuf("xzc", (128, 48), BF16)
        sbuf("dA", (128, 2048), F32)
        sbuf("fixt", (128, 4), F32)
        sbuf("dBx", (128, 2 * 4096), BF16)
        sbuf("h_b", (128, 4096), BF16)
        sbuf("carry", (128, 128), F32)
        sbuf("y_b", (128, 2048), BF16)
        sbuf("s_f", (128, 2080), F32)
        sbuf("g_f", (128, 2048), F32)
        sbuf("hsh", (128, 1024), BF16)
        sbuf("gl_f", (8, 256), F32)
        sbuf("st_f", (128, 16), F32)
        sbuf("m8", (128, 16), F32)
        sbuf("den", (128, 2), F32)
        sbuf("rden", (128, 2), F32)
        sbuf("ind", (128, 16), F32)
        sbuf("gt", (128, 16), F32)
        sbuf("gcm", (8, 256), BF16)
        sbuf("ge_all", (128, 2048), BF16)
        sbuf("stage", (128, 1024), F32)
        sbuf("sb_in", (128, 4 * 1024), BF16)
        sbuf("sb_cd", (128, 2 * 1024), BF16)
        sbuf("sb_w1", (128, 2 * 2048), BF16)
        sbuf("sb_w2", (128, 2 * 2048), BF16)
        sbuf("sb_out", (128, 2 * 2048), BF16)
        sbuf("ws1_b", (128, 4096), BF16)
        sbuf("ws2_b", (128, 4096), BF16)
        sbuf("ident_f", (128, 128), F32)
        sbuf("ident_b", (128, 128), BF16)
        sbuf("ones1", (1, 128), F32)
        sbuf("onescol", (128, 1), BF16)
        sbuf("A_neg", (128, 128), F32)
        sbuf("dtb", (128, 8), F32)
        sbuf("b1c", (128, 32), F32)
        sbuf("bs1c", (128, 4), F32)
        sbuf("bs2c", (128, 8), F32)
        sbuf("rsh", (128, 1), F32)
        sbuf("rs_c", (128, 1), F32)
        sbuf("xpw", (128, 768), BF16)
        sbuf("dtw", (64, 1024), BF16)
        sbuf("gw", (128, 64), BF16)
        sbuf("b2_b", (8, 1024), BF16)
        sbuf("dd_b", (128, 1024), BF16)
        sbuf("cdb", (128, 8), F32)

        ps = {}
        for b in range(8):
            ps["P%d" % b] = En(nc.psum_tensor("P%d" % b, [128, 512], F32))

        sems = {}
        for e in ENGINES:
            sems[ENG_SEM[e]] = En(nc.semaphore(ENG_SEM[e]))
        for s in ("dx", "din", "din1", "dcd", "dcd1", "dw1", "dw1b", "dw2",
                  "dw2b", "dout", "dout1", "dsm", "dcast", "dbc", "dst", "dws",
                  "dbb0", "dbb1", "dbb2", "dbb3", "dge0", "dge1", "dcast0", "dcast1",
                  "dcast2", "dcast3", "dcast4", "dcast5", "dcast6",
                  "dsm1", "dsm2", "dsm3", "dsm4", "dsm5"):
            sems[s] = En(nc.semaphore(s))

        S = Sched()
        global LAST_SCHED
        LAST_SCHED = S
        S.record = [] if RECORD else None

        def apv(t_handle, offset, dims):
            base = t_handle[:, :] if t_handle.shape[0] == 128 else t_handle[0:t_handle.shape[0], :]
            return bass.AP(base.tensor, offset, [list(base.ap[0])] + [list(d) for d in dims])

        bank_open = {}

        def mm(out_ap, lhsT, rhs, bank, first, last, reads, tag=""):
            def fn(eng, out_ap=out_ap, lhsT=lhsT, rhs=rhs, first=first, last=last):
                return eng.matmul(out_ap, lhsT, rhs, start=first, stop=last,
                                  skip_group_check=True)
            S.add("pe", fn, reads=tuple(reads), writes=(bank,), tag=tag)

        def load(name_sb, src_ap, key, region=None, dma="dsm", engine="sp"):
            dst = sb[name_sb][:, :] if region is None else region
            def fn(eng, dst=dst, src=src_ap):
                return eng.dma_start(out=dst, in_=src)
            S.add(engine, fn, reads=(), writes=(key,), dma=dma, tag="ld:" + key)

        def fx0(eng):
            dst = apv(sb["x_f"], 0, [[258, 8], [1, min(T + 2, L)]])
            return eng.dma_start(out=dst, in_=xre[:, :, 0:min(T + 2, L)])
        S.add("sp", fx0, reads=(), writes=("x:p0",), dma="dx", tag="ldx0")
        load("ones1", ap["cones1"], "ones1", region=sb["ones1"][0:1, :], dma="dsm1")
        load("ident_f", ap["cident"], "ident_f", dma="dsm2")
        load("ident_b", ap["cidentb"], "ident_b", dma="dsm3")
        load("onescol", ap["conesc"], "onescol", dma="dsm4")
        load("dtb", ap["aw_dtb"], "dtb", dma="dsm5")
        load("b1c", ap["aw_b1"], "b1c", dma="dsm2")
        load("bs1c", ap["aw_bs1"], "bs1c", dma="dsm3")
        load("bs2c", ap["aw_bs2"], "bs2c", dma="dsm4")
        load("rs_c", ap["aw_rs"], "rs_c", dma="dsm5")

        def cast_small(src_name, dst_name, dst_region, cols, key):
            load("stage", ap[src_name], "stage:0",
                 region=sb["stage"][0:ap[src_name].shape[0], 0:cols])
            def fn(eng, dst=dst_region, src=sb["stage"][0:ap[src_name].shape[0], 0:cols]):
                return eng.activation(dst, src, AF.Copy)
            S.add("act", fn, reads=("stage:0",), writes=(key,), tag="cast:" + key)

        S.add("gp", lambda eng: eng.memset(sb["xzh"][:, :], 0.0),
              reads=(), writes=("xzh",), tag="xzh0")
        S.add("gp", lambda eng: eng.memset(sb["xzc"][:, :], 0.0),
              reads=(), writes=("xzc",), tag="xzc0")
        CONSTS = [1e-5, 1.0, -0.5, -1.0, 0.5]
        for k, v in enumerate(CONSTS):
            nc.const_aps.register(F32, v, sb["cdb"][:, k:k + 1])
            def fmc(eng, k=k, v=v):
                return eng.memset(sb["cdb"][:, k:k + 1], v)
            S.add("gp", fmc, reads=(), writes=("cdb",), tag="cdb%d" % k)

        def cast_blob(ci, nm_src, nm_blob, reads=()):
            def fn(eng, src=ap[nm_src], dst=blobs[nm_blob]):
                return eng.dma_start(out=dst, in_=src)
            S.add("gp", fn, reads=tuple(reads), writes=(nm_blob,),
                  dma="dcast%d" % ci, tag="cast:" + nm_blob)
        cast_blob(0, "w_in", "blob_in")
        cast_blob(1, "w_cd", "blob_cd")

        def late_setup():
            load("stage", ap["aw_alog"], "stage:0", region=sb["stage"][:, 0:128])
            S.add("act", lambda eng: eng.activation(sb["A_neg"][:, :], sb["stage"][:, 0:128], AF.Exp),
                  reads=("stage:0",), writes=("A_neg",), tag="Aexp")
            S.add("dve", lambda eng: eng.tensor_scalar(sb["A_neg"][:, :], sb["A_neg"][:, :],
                                                       -1.0, None, ALU.mult),
                  reads=("A_neg",), writes=("A_neg",), tag="Aneg")
            cast_small("aw_gw", "gw", sb["gw"][:, :], 64, "gw")
            cast_small("aw_dtw", "dtw", sb["dtw"][0:64, :], 1024, "dtw")
            cast_small("aw_xpw", "xpw", sb["xpw"][:, :], 768, "xpw")
            cast_small("aw_b2", "b2_b", sb["b2_b"][0:8, :], 1024, "b2_b")
            cast_small("aw_dd", "dd_b", sb["dd_b"][:, :], 1024, "dd_b")
            S.add("dve", lambda eng: eng.tensor_scalar(sb["rsh"][:, :], sb["rs_c"][:, :],
                                                       0.5, None, ALU.mult),
                  reads=("rs_c",), writes=("rsh",), tag="rsh")

        def late_setup_b():
            cast_blob(2, "w_w1", "blob_w1")
            cast_blob(3, "w_w2", "blob_w2")
            cast_blob(4, "w_out", "blob_out")
            cast_blob(5, "w_ws1", "blob_ws1")
            cast_blob(6, "w_ws2", "blob_ws2")
            def fnws1(eng):
                return eng.dma_start(out=sb["ws1_b"][:, :].rearrange("p (kc h) -> p kc h", kc=8),
                                     in_=blobs["blob_ws1"].rearrange("kc p h -> p kc h"))
            S.add("sp", fnws1, reads=("blob_ws1",), writes=("ws1_b",), dma="dws", tag="ld:ws1")
            def fnws2(eng):
                return eng.dma_start(out=sb["ws2_b"][:, :].rearrange("p (kc d) -> p kc d", kc=4),
                                     in_=blobs["blob_ws2"].rearrange("kc p d -> p kc d"))
            S.add("sp", fnws2, reads=("blob_ws2",), writes=("ws2_b",), dma="dws", tag="ld:ws2")

        xre = ap["x_t"].rearrange("(g p) t -> p g t", p=128)

        def prefetch(i):
            par = i % 2
            c0 = i * T
            wld = min(T + 2, L - c0)
            def fx(eng, c0=c0, wld=wld, par=par):
                dst = apv(sb["x_f"], par * 2064, [[258, 8], [1, wld]])
                return eng.dma_start(out=dst, in_=xre[:, :, c0:c0 + wld])
            S.add("sp", fx, reads=(), writes=("x:p%d" % par,), dma="dx", tag="ldx%d" % i)

        def prefetch_out(i):
            for kcp in range(8):
                def f(eng, kcp=kcp, par=kcp % 2):
                    dst = sb["sb_out"][:, par * 2048:(par + 1) * 2048].rearrange(
                        "p (kc d) -> p kc d", kc=2)
                    return eng.dma_start(out=dst, in_=blobs["blob_out"][kcp * 2:(kcp + 1) * 2, :, :].rearrange("kc p d -> p kc d"))
                S.add("sp", f, reads=("blob_out",), writes=("sbout:p%d" % (kcp % 2),),
                      dma="dout", tag="ldout%d.%d" % (i, kcp))

        def ln_stats(i):
            par = i % 2
            xk = "x:p%d" % par
            xoff = par * 2064
            if i == NCH - 1:
                def f(eng, xoff=xoff):
                    return eng.memset(apv(sb["x_f"], xoff + 256, [[258, 8], [1, 2]]), 0.0)
                S.add("dve", f, reads=(), writes=(xk,), tag="xpad%d" % i)
            def fxb(eng, xoff=xoff):
                return eng.tensor_copy(sb["scrb1"][:, 0:2064], apv(sb["x_f"], xoff, [[1, 2064]]))
            S.add("dve", fxb, reads=(xk,), writes=("scrb1:p0", "scrb1:p1"), tag="xb%d" % i)
            def fxq(eng, xoff=xoff):
                return eng.activation(sb["scrb2"][:, 0:2064], apv(sb["x_f"], xoff, [[1, 2064]]),
                                      AF.Square)
            S.add("act", fxq, reads=(xk,), writes=("scrb2:p0", "scrb2:p1"), tag="xsq%d" % i)
            for g in range(GD):
                mm(ps["P0"][0:1, 0:258], sb["onescol"][:, :],
                   sb["scrb1"][:, g * 258:(g + 1) * 258], "P0", g == 0, g == 7,
                   ["scrb1:p0", "scrb1:p1", "onescol"], tag="lnsum%d" % i)
            def fm(eng):
                return eng.tensor_scalar(sb["strow"][0:1, 516:774], ps["P0"][0:1, 0:258],
                                         1.0 / 1024.0, None, ALU.mult)
            S.add("dve", fm, reads=("P0",), writes=("strow_m",), tag="lnm%d" % i)
            for g in range(GD):
                mm(ps["P1"][0:1, 0:258], sb["onescol"][:, :],
                   sb["scrb2"][:, g * 258:(g + 1) * 258], "P1", g == 0, g == 7,
                   ["scrb2:p0", "scrb2:p1", "onescol"], tag="lnssq%d" % i)
            def fv1(eng):
                return eng.tensor_scalar(sb["strow"][0:1, 774:1032], ps["P1"][0:1, 0:258],
                                         1.0 / 1024.0, None, ALU.mult)
            S.add("dve", fv1, reads=("P1",), writes=("strow_v",), tag="lnv1%d" % i)
            def fmsq(eng):
                return eng.tensor_tensor(sb["strow"][0:1, 0:258], sb["strow"][0:1, 516:774],
                                         sb["strow"][0:1, 516:774], ALU.mult)
            S.add("dve", fmsq, reads=("strow_m",), writes=("strow_a",), tag="lnmsq%d" % i)
            def fv2(eng):
                return eng.tensor_tensor(sb["strow"][0:1, 774:1032], sb["strow"][0:1, 774:1032],
                                         sb["strow"][0:1, 0:258], ALU.subtract)
            S.add("dve", fv2, reads=("strow_v", "strow_a"), writes=("strow_v",), tag="lnv2%d" % i)
            def fln(eng):
                return eng.activation(sb["strow"][0:1, 774:1032], sb["strow"][0:1, 774:1032],
                                      AF.Ln, bias=1e-5)
            S.add("act", fln, reads=("strow_v", "cdb"), writes=("strow_v",), tag="lnln%d" % i)
            def fex(eng):
                return eng.activation(sb["strow"][0:1, 0:258], sb["strow"][0:1, 774:1032],
                                      AF.Exp, scale=-0.5)
            S.add("act", fex, reads=("strow_v", "cdb"), writes=("strow_a",), tag="lnex%d" % i)
            def fb1(eng):
                return eng.tensor_tensor(sb["strow"][0:1, 258:516], sb["strow"][0:1, 0:258],
                                         sb["strow"][0:1, 516:774], ALU.mult)
            S.add("dve", fb1, reads=("strow_a", "strow_m"), writes=("strow_b",), tag="lnb1%d" % i)
            def fb2(eng):
                return eng.tensor_scalar(sb["strow"][0:1, 258:516], sb["strow"][0:1, 258:516],
                                         -1.0, None, ALU.mult)
            S.add("dve", fb2, reads=("strow_b",), writes=("strow_b",), tag="lnb2%d" % i)
            mm(ps["P0"][:, 0:258], sb["ones1"][0:1, :], sb["strow"][0:1, 0:258],
               "P0", True, True, ["ones1", "strow_a"], tag="lnbca%d" % i)
            def fea(eng):
                return eng.activation(sb["lnab"][:, 0:258], ps["P0"][:, 0:258], AF.Copy)
            S.add("act", fea, reads=("P0",), writes=("lnab_a",), tag="lnea%d" % i)
            mm(ps["P1"][:, 0:258], sb["ones1"][0:1, :], sb["strow"][0:1, 258:516],
               "P1", True, True, ["ones1", "strow_b"], tag="lnbcb%d" % i)
            def feb(eng):
                return eng.activation(sb["lnab"][:, 258:516], ps["P1"][:, 0:258], AF.Copy)
            S.add("act", feb, reads=("P1",), writes=("lnab_b",), tag="lneb%d" % i)
        def ln_xn(i):
            par = i % 2
            xk = "x:p%d" % par
            xoff = par * 2064
            a_bc = apv(sb["lnab"], 0, [[0, 8], [1, 258]])
            b_bc = apv(sb["lnab"], 258, [[0, 8], [1, 258]])
            skeys = tuple("s:%d" % j for j in range(8))
            def fx1(eng, xoff=xoff, a_bc=a_bc):
                return eng.tensor_tensor(apv(sb["s_f"], 0, [[258, 8], [1, 258]]),
                                         apv(sb["x_f"], xoff, [[258, 8], [1, 258]]),
                                         a_bc, ALU.mult)
            S.add("dve", fx1, reads=(xk, "lnab_a"), writes=skeys, tag="xn1_%d" % i)
            def fx2(eng, b_bc=b_bc):
                return eng.tensor_tensor(apv(sb["xn_b"], 0, [[258, 8], [1, 258]]),
                                         apv(sb["s_f"], 0, [[258, 8], [1, 258]]),
                                         b_bc, ALU.add)
            S.add("dve", fx2, reads=skeys + ("lnab_b",), writes=("xn",), tag="xn2_%d" % i)

        def out_proj_phase_a(i):
            for kcp in (0, 1, 2, 4, 5, 6, 7):
                par = kcp % 2
                def fld(eng, kcp=kcp, par=par):
                    dst = sb["sb_out"][:, par * 2048:(par + 1) * 2048].rearrange(
                        "p (kc d) -> p kc d", kc=2)
                    return eng.dma_start(out=dst, in_=blobs["blob_out"][kcp * 2:(kcp + 1) * 2, :, :].rearrange("kc p d -> p kc d"))
                S.add("sp", fld, reads=("blob_out",), writes=("sbout:p%d" % par,),
                      dma="dout" if kcp % 2 == 0 else "dout1", tag="ldout%d.%d" % (i, kcp))
                for kci in range(2):
                    kc = kcp * 2 + kci
                    for j in range(GD):
                        bank = "P%d" % (4 + j // 2)
                        half = (j % 2) * 256
                        lhsT = sb["sb_out"][:, par * 2048 + kci * 1024 + j * 128:
                                            par * 2048 + kci * 1024 + (j + 1) * 128]
                        if kc < 8:
                            rhs = sb["y_b"][:, kc * 256:(kc + 1) * 256]
                            rk = "y"
                        else:
                            rhs = sb["zc"][:, (kc - 8) * 256:(kc - 7) * 256]
                            rk = "zc"
                        mm(ps[bank][:, half:half + 256], lhsT, rhs, bank,
                           (j % 2 == 0) and kc == 0, False,
                           ["sbout:p%d" % par, rk], tag="op%d.%d.%d" % (i, j, kc))
        def out_proj_a_evac(i, js):
            done = set()
            for j in js:
                q = j // 2
                if q in done:
                    continue
                done.add(q)
                bank = "P%d" % (4 + q)
                def fe(eng, q=q, bank=bank):
                    return eng.activation(sb["s_f"][:, q * 512:(q + 1) * 512],
                                          ps[bank][:, :], AF.Copy)
                S.add("act", fe, reads=(bank,),
                      writes=("s:%d" % (2 * q), "s:%d" % (2 * q + 1)),
                      tag="ope%d.%d" % (i, 2 * q))

        def out_proj_phase_b(i):
            for kcp in (3,):
                par = kcp % 2
                def fld(eng, kcp=kcp, par=par):
                    dst = sb["sb_out"][:, par * 2048:(par + 1) * 2048].rearrange(
                        "p (kc d) -> p kc d", kc=2)
                    return eng.dma_start(out=dst, in_=blobs["blob_out"][kcp * 2:(kcp + 1) * 2, :, :].rearrange("kc p d -> p kc d"))
                S.add("sp", fld, reads=("blob_out",), writes=("sbout:p%d" % par,),
                      dma="dout" if kcp % 2 == 0 else "dout1", tag="ldoutb%d.%d" % (i, kcp))
                for kci in range(2):
                    kc = kcp * 2 + kci
                    for j in range(GD):
                        bank = "P%d" % (4 + j // 2)
                        half = (j % 2) * 256
                        lhsT = sb["sb_out"][:, par * 2048 + kci * 1024 + j * 128:
                                            par * 2048 + kci * 1024 + (j + 1) * 128]
                        rhs = sb["y_b"][:, kc * 256:(kc + 1) * 256]
                        mm(ps[bank][:, half:half + 256], lhsT, rhs, bank,
                           (j % 2 == 0) and kci == 0, False,
                           ["sbout:p%d" % par, "y"], tag="opb%d.%d.%d" % (i, j, kc))
            for q in range(4):
                bank = "P%d" % (4 + q)
                def fa(eng, q=q, bank=bank):
                    v = sb["s_f"][:, q * 512:(q + 1) * 512]
                    return eng.tensor_tensor(v, v, ps[bank][:, 0:512], ALU.add)
                S.add("dve", fa, reads=(bank, "s:%d" % (2 * q), "s:%d" % (2 * q + 1)),
                      writes=("s:%d" % (2 * q), "s:%d" % (2 * q + 1)), tag="opa%d.%d" % (i, q))

        def combine_phase(i):
            par = i % 2
            def f1(eng):
                return eng.tensor_tensor(sb["g_f"][:, :], sb["s_f"][:, 0:2048],
                                         sb["g_f"][:, :], ALU.mult)
            S.add("dve", f1, reads=tuple(["s:%d" % j for j in range(8)]) + ("g",),
                  writes=("g",), tag="cmb1_%d" % i)
            def f2(eng, par=par):
                return eng.tensor_tensor(sb["s_f"][:, 0:2048], sb["g_f"][:, :],
                                         apv(sb["x_f"], par * 2064, [[258, 8], [1, 256]]),
                                         ALU.add)
            S.add("dve", f2, reads=("g", "x:p%d" % par) + tuple(["s:%d" % j for j in range(8)]),
                  writes=tuple(["s:%d" % j for j in range(8)]), tag="cmb2_%d" % i)

        def store_phase(i):
            for tb in range(2):
                for gq in range(2):
                    bank = "P6" if gq == 0 else "P7"
                    for gg in range(4):
                        g = gq * 4 + gg
                        def ft(eng, g=g, gg=gg, tb=tb, bank=bank):
                            return eng.matmul(
                                bass.AP(ps[bank][:, :].tensor, gg * 128, [list(ps[bank][:, :].ap[0]), [1, 128]]),
                                sb["s_f"][:, g * 256 + tb * 128: g * 256 + (tb + 1) * 128],
                                sb["ident_f"][:, :], start=(gg == 0), stop=(gg == 3),
                                is_transpose=True, skip_group_check=True)
                        S.add("pe", ft, reads=["s:%d" % g, "ident_f"], writes=(bank,),
                              tag="tr%d.%d.%d" % (i, tb, g))
                    def fe(eng, gq=gq, tb=tb, bank=bank):
                        return eng.activation(sb["stage"][:, gq * 512:(gq + 1) * 512],
                                              ps[bank][:, :], AF.Copy)
                    S.add("act", fe, reads=(bank,), writes=("stage:0",),
                          tag="tre%d.%d.%d" % (i, tb, gq))
                def fs(eng, tb=tb, i=i):
                    return eng.dma_start(
                        out=ap["out"][i * T + tb * 128: i * T + (tb + 1) * 128, :],
                        in_=sb["stage"][:, 0:1024])
                S.add("sp", fs, reads=("stage:0",), writes=(), dma="dst",
                      tag="st%d.%d" % (i, tb))

        def in_proj_phase(i, jh):
            if jh == 0:
                def fc(eng):
                    dst = apv(sb["xzh"], 0, [[259, 16], [1, 3]])
                    return eng.tensor_copy(dst, sb["xzc"][:, :].rearrange("p (b c) -> p b c", b=16))
                S.add("dve", fc, reads=("xzc",), writes=("xzh",), tag="xzc%d" % i)
            for kc in range(8):
                par = (jh * 8 + kc) % 4
                def fld(eng, kc=kc, jh=jh, par=par):
                    dst = sb["sb_in"][:, par * 1024:(par + 1) * 1024]
                    return eng.dma_start(out=dst, in_=blobs["blob_in"][kc, :, jh * 1024:(jh + 1) * 1024])
                S.add("sp", fld, reads=("blob_in",), writes=("sbin:p%d" % par,),
                      dma="din" if kc % 2 == 0 else "din1",
                      tag="ldin%d.%d.%d" % (i, kc, jh))
                for j in range(8):
                    jg = jh * 8 + j
                    bank = "P%d" % (j // 2)
                    quarter = (j % 2) * 256
                    lhsT = sb["sb_in"][:, par * 1024 + j * 128: par * 1024 + (j + 1) * 128]
                    rhs = apv(sb["xn_b"], kc * 258 + 2, [[1, 256]])
                    mm(ps[bank][:, quarter:quarter + 256], lhsT, rhs, bank,
                       kc == 0 and (j % 2 == 0), False,
                       ["sbin:p%d" % par, "xn"], tag="ip%d.%d.%d.%d" % (i, jh, kc, j))
                if i == 0:
                    for j in range(8):
                        jg = jh * 8 + j
                        lhsT = sb["sb_in"][:, par * 1024 + j * 128: par * 1024 + (j + 1) * 128]
                        rhs = apv(sb["xn_b"], kc * 258, [[1, 2]])
                        mm(ps["P7"][:, jg * 2:(jg + 1) * 2], lhsT, rhs, "P7",
                           kc == 0 and jg == 0 and jh == 0, False,
                           ["sbin:p%d" % par, "xn"], tag="iph%d.%d" % (kc, jg))
            for b in range(4):
                j0 = jh * 8 + b * 2
                def fe(eng, b=b, j0=j0):
                    dst = apv(sb["xzh"], j0 * 259 + 3, [[259, 2], [1, 256]])
                    return eng.activation(dst, ps["P%d" % b][:, :].rearrange(
                        "p (b2 t) -> p b2 t", b2=2), AF.Copy)
                S.add("act", fe, reads=("P%d" % b,), writes=("xzh",),
                      tag="ipe%d.%d.%d" % (i, jh, b))
            if i == 0:
                def feh(eng, jh=jh):
                    dst = apv(sb["xzh"], jh * 8 * 259 + 1, [[259, 8], [1, 2]])
                    return eng.activation(dst, ps["P7"][:, jh * 16:jh * 16 + 16].rearrange(
                        "p (b2 t) -> p b2 t", b2=8), AF.Copy)
                S.add("act", feh, reads=("P7",), writes=("xzh",), tag="ipeh%d" % jh)
            if i == NCH - 1:
                def fz(eng, jh=jh):
                    return eng.memset(apv(sb["xzh"], jh * 8 * 259 + 257, [[259, 8], [1, 2]]), 0.0)
                S.add("dve", fz, reads=("xzh",), writes=("xzh",), tag="xzz%d.%d" % (i, jh))

        def conv_phase(i, br0):
            for br in (br0,):
                def ld(k):
                    par = (br * 4 + k) % 2
                    def fld(eng, br=br, k=k, par=par):
                        dst = sb["sb_cd"][:, par * 1024:(par + 1) * 1024]
                        return eng.dma_start(out=dst, in_=blobs["blob_cd"][br, k, :, :])
                    S.add("sp", fld, reads=("blob_cd",), writes=("sbcd:p%d" % par,),
                          dma="dcd" if k % 2 == 0 else "dcd1", tag="ldcd%d.%d.%d" % (i, br, k))
                ld(0)
                ld(1)
                for k in range(4):
                    par = (br * 4 + k) % 2
                    for g in range(GD):
                        bank = "P%d" % (g // 2)
                        half = (g % 2) * 256
                        blk = br * 8 + g
                        lhsT = sb["sb_cd"][:, par * 1024 + g * 128: par * 1024 + (g + 1) * 128]
                        rhs = apv(sb["xzh"], blk * 259 + k, [[1, 256]])
                        mm(ps[bank][:, half:half + 256], lhsT, rhs, bank,
                           k == 0 and (g % 2 == 0), False,
                           ["sbcd:p%d" % par, "xzh"], tag="cv%d.%d.%d.%d" % (i, br, k, g))
                    if k + 2 < 4:
                        ld(k + 2)
                for b in range(4):
                    if br == 0:
                        if b < 2:
                            dst = sb["xc"][:, b * 512:(b + 1) * 512]
                            wkey = "xc"
                        else:
                            par = i % 2
                            dst = sb["xco"][:, par * 1024 + (b - 2) * 512:
                                            par * 1024 + (b - 1) * 512]
                            wkey = "xco:p%d" % par
                    else:
                        dst = sb["zc"][:, b * 512:(b + 1) * 512]
                        wkey = "zc"
                    def fe(eng, b=b, dst=dst):
                        return eng.activation(dst, ps["P%d" % b][:, :], AF.Silu)
                    S.add("act", fe, reads=("P%d" % b,), writes=(wkey,),
                          tag="cve%d.%d.%d" % (i, br, b))

        def xz_tail_save(i, br0):
            def fts(eng, br0=br0):
                srcv = apv(sb["xzh"], br0 * 8 * 259 + 256, [[259, 8], [1, 3]])
                return eng.tensor_copy(
                    bass.AP(sb["xzc"][:, :].tensor, br0 * 24, [list(sb["xzc"][:, :].ap[0]), [3, 8], [1, 3]]),
                    srcv)
            S.add("dve", fts, reads=("xzh",), writes=("xzc",), tag="xzts%d.%d" % (i, br0))

        def xproj_phase(i):
            for kc in range(8):
                lhsT = sb["xpw"][:, kc * 96:(kc + 1) * 96]
                if kc < 4:
                    rhs = sb["xc"][:, kc * 256:(kc + 1) * 256]
                    rkey = "xc"
                else:
                    par = i % 2
                    rhs = sb["xco"][:, par * 1024 + (kc - 4) * 256:
                                    par * 1024 + (kc - 3) * 256]
                    rkey = "xco:p%d" % par
                mm(ps["P4"][0:96, 0:256], lhsT, rhs, "P4", kc == 0, kc == 7,
                   ["xpw", rkey], tag="xp%d.%d" % (i, kc))
            def f1(eng):
                return eng.activation(sb["dtr"][0:64, :], ps["P4"][0:64, 0:256], AF.Copy)
            S.add("act", f1, reads=("P4",), writes=("dtr",), tag="xpe1_%d" % i)
            def f2(eng):
                return eng.activation(sb["stage"][0:32, 768:1024], ps["P4"][64:96, 0:256], AF.Copy)
            S.add("act", f2, reads=("P4", "stage:0"), writes=("stage:0",), tag="xpe2_%d" % i)
            def f3(eng):
                return eng.activation(sb["bcB"][0:16, :], sb["stage"][0:16, 768:1024], AF.Copy)
            S.add("act", f3, reads=("stage:0",), writes=("bcB_B",), tag="xpe3_%d" % i)
            def f4(eng):
                return eng.dma_start(out=sb["bcB"][32:48, :], in_=sb["stage"][16:32, 768:1024])
            S.add("gp", f4, reads=("stage:0",), writes=("bcB_C",), dma="dbc", tag="xpe4_%d" % i)

        def dt_phase(i, js=None):
            for j in (range(GD) if js is None else js):
                bank = ("P4", "P5", "P6")[j % 3]
                half = 0
                lhsT = sb["dtw"][0:64, j * 128:(j + 1) * 128]
                mm(ps[bank][0:128, 0:256], lhsT, sb["dtr"][0:64, :], bank,
                   True, True, ["dtw", "dtr"], tag="dt%d.%d" % (i, j))
                scr = sb["stage"][:, (j % 4) * 256:(j % 4 + 1) * 256]
                def f1(eng, j=j, bank=bank, scr=scr):
                    return eng.activation(scr, ps[bank][:, 0:256], AF.Abs,
                                          bias=sb["dtb"][:, j:j + 1])
                S.add("act", f1, reads=(bank, "dtb", "stage:0"), writes=("stage:0",),
                      tag="dte1_%d.%d" % (i, j))
                def f4(eng, j=j, bank=bank):
                    return eng.activation(sb["delta"][:, j * 256:(j + 1) * 256],
                                          ps[bank][:, 0:256], AF.Relu,
                                          bias=sb["dtb"][:, j:j + 1])
                S.add("act", f4, reads=(bank, "dtb"), writes=("delta:%d" % j,), tag="dte4_%d.%d" % (i, j))
                def f2(eng, j=j, scr=scr):
                    return eng.activation(scr, scr, AF.Exp, scale=-1.0)
                S.add("act", f2, reads=("stage:0", "cdb"), writes=("stage:0",), tag="dte2_%d.%d" % (i, j))
                def f3(eng, j=j, scr=scr):
                    return eng.activation(scr, scr, AF.Ln, bias=1.0)
                S.add("act", f3, reads=("stage:0", "cdb"), writes=("stage:0",), tag="dte3_%d.%d" % (i, j))
                def f5(eng, j=j, scr=scr):
                    d = sb["delta"][:, j * 256:(j + 1) * 256]
                    return eng.tensor_tensor(d, d, scr, ALU.add)
                S.add("gp", f5, reads=("stage:0", "delta:%d" % j), writes=("delta:%d" % j,),
                      tag="dte5_%d.%d" % (i, j))
                if j < 4:
                    xsrc = sb["xc"][:, j * 256:(j + 1) * 256]
                    xkey = "xc"
                else:
                    par = i % 2
                    xsrc = sb["xco"][:, par * 1024 + (j - 4) * 256:
                                     par * 1024 + (j - 3) * 256]
                    xkey = "xco:p%d" % par
                def fu(eng, j=j, xsrc=xsrc):
                    return eng.tensor_tensor(sb["u_b"][:, j * 256:(j + 1) * 256],
                                             sb["delta"][:, j * 256:(j + 1) * 256],
                                             xsrc, ALU.mult)
                S.add("gp", fu, reads=("delta:%d" % j, xkey), writes=("u:%d" % j,), tag="u%d.%d" % (i, j))

        def gate_phase(i):
            for kc in range(8):
                lhsT = sb["gw"][:, kc * 8:(kc + 1) * 8]
                rhs = sb["xn_b"][:, kc * 258:kc * 258 + 256]
                mm(ps["P6"][0:8, 0:256], lhsT, rhs, "P6", kc == 0, kc == 7,
                   ["gw", "xn"], tag="gl%d.%d" % (i, kc))
            def ft(eng):
                return eng.activation(sb["gl_f"][0:8, :], ps["P6"][0:8, 0:256], AF.Tanh,
                                      scale=0.5)
            S.add("act", ft, reads=("P6", "cdb"), writes=("gl",), tag="gt%d" % i)
            for tb in range(2):
                def f(eng, tb=tb):
                    return eng.matmul(
                        bass.AP(ps["P6"][:, :].tensor, 256 + tb * 8, [list(ps["P6"][:, :].ap[0]), [1, 8]]),
                        sb["gl_f"][0:8, tb * 128:(tb + 1) * 128], sb["ident_f"][0:8, 0:8],
                        start=(tb == 0), stop=(tb == 1), is_transpose=True,
                        skip_group_check=True)
                S.add("pe", f, reads=("gl", "ident_f"), writes=("P6",), tag="gtr%d.%d" % (i, tb))
            def fs(eng):
                return eng.tensor_scalar(sb["st_f"][:, :], ps["P6"][:, 256:272],
                                         sb["rsh"][:, :], sb["rsh"][:, :], ALU.mult, ALU.add)
            S.add("dve", fs, reads=("P6", "rsh"), writes=("st",), tag="gs%d" % i)
            for tb in range(2):
                def f8(eng, tb=tb):
                    return eng.max(sb["m8"][:, tb * 8:(tb + 1) * 8], sb["st_f"][:, tb * 8:(tb + 1) * 8])
                S.add("dve", f8, reads=("st",), writes=("m8",), tag="gm8_%d.%d" % (i, tb))
            def fd(eng):
                return eng.tensor_tensor(sb["den"][:, :], apv(sb["m8"], 0, [[8, 2]]),
                                         apv(sb["m8"], 1, [[8, 2]]), ALU.add)
            S.add("dve", fd, reads=("m8",), writes=("den",), tag="gden%d" % i)
            def fr(eng):
                return eng.reciprocal(sb["rden"][:, :], sb["den"][:, :])
            S.add("dve", fr, reads=("den",), writes=("rden",), tag="grec%d" % i)
            for tb in range(2):
                def fi(eng, tb=tb):
                    return eng.tensor_scalar(sb["ind"][:, tb * 8:(tb + 1) * 8],
                                             sb["st_f"][:, tb * 8:(tb + 1) * 8],
                                             sb["m8"][:, tb * 8 + 1:tb * 8 + 2], None,
                                             ALU.is_ge)
                S.add("dve", fi, reads=("st", "m8"), writes=("ind",), tag="gind%d.%d" % (i, tb))
                def fg1(eng, tb=tb):
                    return eng.tensor_scalar(sb["gt"][:, tb * 8:(tb + 1) * 8],
                                             sb["st_f"][:, tb * 8:(tb + 1) * 8],
                                             sb["rden"][:, tb:tb + 1], None, ALU.mult)
                S.add("dve", fg1, reads=("st", "rden"), writes=("gt",), tag="gg1_%d.%d" % (i, tb))
            def fg2(eng):
                return eng.tensor_tensor(sb["gt"][:, :], sb["gt"][:, :], sb["ind"][:, :],
                                         ALU.mult)
            S.add("dve", fg2, reads=("gt", "ind"), writes=("gt",), tag="gg2_%d" % i)
            for tb in range(2):
                def fb(eng, tb=tb):
                    return eng.matmul(
                        ps["P6"][0:8, tb * 128:(tb + 1) * 128],
                        sb["gt"][:, tb * 8:(tb + 1) * 8], sb["ident_f"][:, :],
                        start=(tb == 0), stop=(tb == 1), is_transpose=True,
                        skip_group_check=True)
                S.add("pe", fb, reads=("gt", "ident_f"), writes=("P6",), tag="gbk%d.%d" % (i, tb))
            def fe(eng):
                return eng.activation(sb["gcm"][0:8, :], ps["P6"][0:8, 0:256], AF.Copy)
            S.add("act", fe, reads=("P6",), writes=("gcm",), tag="gev%d" % i)


        def gate_bcast(i):
            def fgd(eng):
                return eng.dma_start(out=blobs["ge_dram"][0:8, :], in_=sb["gcm"][0:8, :])
            S.add("gp", fgd, reads=("gcm",), writes=("ged",), dma="dge0", tag="ged%d" % i)
            def fgb(eng):
                srcv = bass.AP(blobs["ge_dram"].tensor, 0, [[0, 128], [256, 8], [1, 256]])
                return eng.dma_start(out=sb["ge_all"][:, :].rearrange("p (n t) -> p n t", n=8),
                                     in_=srcv)
            S.add("gp", fgb, reads=("ged",), writes=("ge_all",), dma="dge1", tag="geb%d" % i)

        def bbcb_phase(i):
            def f0(eng):
                return eng.dma_start(out=blobs["bc_dram"][0:16, :], in_=sb["bcB"][0:16, :])
            S.add("gp", f0, reads=("bcB_B",), writes=("bcd",),
                  dma="dbb0", tag="bcd%d" % i)
            def f0c(eng):
                return eng.dma_start(out=blobs["bc_dram"][32:48, :], in_=sb["bcB"][32:48, :])
            S.add("gp", f0c, reads=("bcB_C",), writes=("bcd",),
                  dma="dbb3", tag="bcdc%d" % i)
            for which, base, wk in [(0, 0, "Bb"), (1, 32, "Cb")]:
                def fd(eng, base=base, wk=wk):
                    srcv = bass.AP(blobs["bc_dram"].tensor, base * 256,
                                   [[0, 128], [256, 16], [1, 256]])
                    return eng.dma_start(out=sb[wk][:, :].rearrange("p (n t) -> p n t", n=16),
                                         in_=srcv)
                S.add("gp", fd, reads=("bcd",), writes=(wk,),
                      dma="dbb%d" % (1 + which), tag="bc%d.%d" % (i, which))

        def scan_group(i, g):
            dpar = g % 2
            dko = dpar * 4096
            dk = "dBx:p%d" % dpar
            def fb(eng, g=g, dko=dko):
                u_bc = apv(sb["u_b"], g * 256, [[0, 16], [1, 256]])
                return eng.tensor_tensor(apv(sb["dBx"], dko, [[256, 16], [1, 256]]),
                                         u_bc, apv(sb["Bb"], 0, [[256, 16], [1, 256]]),
                                         ALU.mult)
            S.add("dve", fb, reads=("u:%d" % g, "Bb", dk), writes=(dk,), tag="dbx%d.%d" % (i, g))
            for n in range(DSTATE):
                slot = (g * DSTATE + n) % 8
                def fx(eng, g=g, n=n, slot=slot):
                    return eng.activation(sb["dA"][:, slot * 256:(slot + 1) * 256],
                                          sb["delta"][:, g * 256:(g + 1) * 256],
                                          AF.Exp, scale=sb["A_neg"][:, g * 16 + n:g * 16 + n + 1])
                S.add("act", fx, reads=("delta:%d" % g, "A_neg", "dA:s%d" % slot),
                      writes=("dA:s%d" % slot,), tag="ex%d.%d.%d" % (i, g, n))
                if n % 4 == 3:
                    q0 = n - 3
                    s0 = (g * DSTATE + q0) % 8
                    skeys = tuple("dA:s%d" % ((s0 + k) % 8) for k in range(4))
                    bkeys = skeys[1:]
                    if i > 0:
                        def fm(eng, g=g, q0=q0, s0=s0):
                            return eng.tensor_tensor(
                                sb["fixt"][:, 0:3],
                                apv(sb["dA"], (s0 + 1) * 256, [[256, 3], [1, 1]]),
                                apv(sb["carry"], g * 16 + q0 + 1, [[1, 3], [1, 1]]),
                                ALU.mult)
                        S.add("dve", fm, reads=bkeys + ("carry:g%d" % g, "fixt"),
                              writes=("fixt",), tag="sfm%d.%d.%d" % (i, g, q0))
                        def fa(eng, g=g, q0=q0, dko=dko):
                            v = apv(sb["dBx"], dko + (q0 + 1) * 256, [[256, 3], [1, 1]])
                            return eng.tensor_tensor(v, v, sb["fixt"][:, 0:3], ALU.add)
                        S.add("dve", fa, reads=("fixt", dk), writes=(dk,),
                              tag="sfa%d.%d.%d" % (i, g, q0))
                    def fz(eng, s0=s0):
                        return eng.memset(apv(sb["dA"], (s0 + 1) * 256, [[256, 3], [1, 1]]), 0.0)
                    S.add("dve", fz, reads=bkeys, writes=bkeys, tag="sfz%d.%d.%d" % (i, g, q0))
                    def fsc(eng, g=g, q0=q0, s0=s0, i=i, dko=dko):
                        init = 0.0 if i == 0 else sb["carry"][:, g * 16 + q0:g * 16 + q0 + 1]
                        return eng.tensor_tensor_scan(
                            sb["h_b"][:, q0 * 256:(q0 + 4) * 256],
                            apv(sb["dA"], s0 * 256, [[1, 1024]]),
                            apv(sb["dBx"], dko + q0 * 256, [[1, 1024]]),
                            init, ALU.mult, ALU.add)
                    S.add("dve", fsc, reads=skeys + (dk, "carry:g%d" % g, "h"),
                          writes=("h",) + skeys, tag="sc%d.%d.%d" % (i, g, q0))
            def fcs(eng, g=g):
                src = apv(sb["h_b"], 255, [[256, 16]])
                return eng.tensor_copy(sb["carry"][:, g * 16:(g + 1) * 16], src)
            S.add("dve", fcs, reads=("h",), writes=("carry:g%d" % g,), tag="cs%d.%d" % (i, g))
            def fhc(eng, dko=dko):
                return eng.tensor_tensor(apv(sb["dBx"], dko, [[256, 16], [1, 256]]),
                                         apv(sb["h_b"], 0, [[256, 16], [1, 256]]),
                                         apv(sb["Cb"], 0, [[256, 16], [1, 256]]), ALU.mult)
            S.add("dve", fhc, reads=("h", "Cb", dk), writes=(dk,), tag="hc%d.%d" % (i, g))

        def idents_phase(i, g, first=False):
            dpar = g % 2
            dko = dpar * 4096
            dk = "dBx:p%d" % dpar
            half = dpar * 256
            for n in range(DSTATE):
                mm(ps["P7"][:, half:half + 256], sb["ident_b"][:, :],
                   apv(sb["dBx"], dko + n * 256, [[1, 256]]), "P7",
                   n == 0 and (dpar == 0 or first), False, [dk, "ident_b"],
                   tag="yid%d.%d.%d" % (i, g, n))
            if g < 4:
                xsrc = sb["xc"][:, g * 256:(g + 1) * 256]
                xkey = "xc"
            else:
                par = i % 2
                xsrc = sb["xco"][:, par * 1024 + (g - 4) * 256:
                                 par * 1024 + (g - 3) * 256]
                xkey = "xco:p%d" % par
            mm(ps["P7"][:, half:half + 256], sb["dd_b"][:, g * 128:(g + 1) * 128],
               xsrc, "P7", False, True,
               ["dd_b", xkey], tag="ydd%d.%d" % (i, g))
            def fy(eng, g=g, half=half):
                return eng.activation(sb["y_b"][:, g * 256:(g + 1) * 256],
                                      ps["P7"][:, half:half + 256], AF.Copy)
            S.add("act", fy, reads=("P7",), writes=("y",), tag="ye%d.%d" % (i, g))

        def ws1_phase(i):
            for kc in range(8):
                for hg in range(4):
                    bank = "P4" if hg < 2 else "P5"
                    half = (hg % 2) * 256
                    lhsT = sb["ws1_b"][:, kc * 512 + hg * 128: kc * 512 + (hg + 1) * 128]
                    rhs = sb["xn_b"][:, kc * 258:kc * 258 + 256]
                    mm(ps[bank][:, half:half + 256], lhsT, rhs, bank,
                       kc == 0 and hg % 2 == 0, False, ["ws1_b", "xn"],
                       tag="ws1_%d.%d.%d" % (i, kc, hg))
            for hg in range(4):
                bank = "P4" if hg < 2 else "P5"
                half = (hg % 2) * 256
                def fe(eng, hg=hg, bank=bank, half=half):
                    return eng.activation(sb["hsh"][:, hg * 256:(hg + 1) * 256],
                                          ps[bank][:, half:half + 256], AF.Gelu,
                                          bias=sb["bs1c"][:, hg:hg + 1])
                S.add("act", fe, reads=(bank, "bs1c"), writes=("hsh",), tag="ws1e%d.%d" % (i, hg))

        def ffn_expert(i, e):
            par = e % 2
            for kh in range(2):
                spar = (e * 2 + kh) % 2
                def fld1(eng, e=e, kh=kh, spar=spar):
                    dst = sb["sb_w1"][:, spar * 2048:(spar + 1) * 2048].rearrange(
                        "p (kc h) -> p kc h", kc=4)
                    return eng.dma_start(out=dst, in_=blobs["blob_w1"][e, kh * 4:(kh + 1) * 4, :, :].rearrange("kc p h -> p kc h"))
                S.add("sp", fld1, reads=("blob_w1",), writes=("sbw1:p%d" % spar,),
                      dma="dw1" if spar == 0 else "dw1b", tag="ldw1%d.%d.%d" % (i, e, kh))
                for kci in range(4):
                    kc = kh * 4 + kci
                    for hg in range(4):
                        bank = "P4" if hg < 2 else "P5"
                        half = (hg % 2) * 256
                        lhsT = sb["sb_w1"][:, spar * 2048 + kci * 512 + hg * 128:
                                           spar * 2048 + kci * 512 + (hg + 1) * 128]
                        rhs = sb["xn_b"][:, kc * 258:kc * 258 + 256]
                        mm(ps[bank][:, half:half + 256], lhsT, rhs, bank,
                           kc == 0 and hg % 2 == 0, False,
                           ["sbw1:p%d" % spar, "xn"], tag="w1_%d.%d.%d.%d" % (i, e, kc, hg))
            for hg in range(4):
                bank = "P4" if hg < 2 else "P5"
                half = (hg % 2) * 256
                def fe(eng, e=e, hg=hg, bank=bank, half=half, par=par):
                    return eng.activation(sb["scrb2"][:, par * 1024 + hg * 256:
                                                      par * 1024 + (hg + 1) * 256],
                                          ps[bank][:, half:half + 256], AF.Gelu,
                                          bias=sb["b1c"][:, e * 4 + hg:e * 4 + hg + 1])
                S.add("act", fe, reads=(bank, "b1c"), writes=("scrb2:p%d" % par,),
                      tag="gel%d.%d.%d" % (i, e, hg))
        def hgate_phase(i, e):
            par = e % 2
            def fhg(eng, e=e, par=par):
                geb = apv(sb["ge_all"], e * 256, [[0, 4], [1, 256]])
                return eng.tensor_tensor(
                    apv(sb["scrb1"], par * 1024, [[256, 4], [1, 256]]),
                    apv(sb["scrb2"], par * 1024, [[256, 4], [1, 256]]), geb, ALU.mult)
            S.add("dve", fhg, reads=("scrb2:p%d" % par, "ge_all", "scrb1:p%d" % par),
                  writes=("scrb1:p%d" % par,), tag="hg%d.%d" % (i, e))
        def ffn_w2(i, e):
            par = e % 2
            for kh in range(2):
                spar = (e * 2 + kh) % 2
                def fld2(eng, e=e, kh=kh, spar=spar):
                    dst = sb["sb_w2"][:, spar * 2048:(spar + 1) * 2048].rearrange(
                        "p (kc d) -> p kc d", kc=2)
                    return eng.dma_start(out=dst, in_=blobs["blob_w2"][e, kh * 2:(kh + 1) * 2, :, :].rearrange("kc p d -> p kc d"))
                S.add("sp", fld2, reads=("blob_w2",), writes=("sbw2:p%d" % spar,),
                      dma="dw2" if spar == 0 else "dw2b", tag="ldw2%d.%d.%d" % (i, e, kh))
                for kci in range(2):
                    kc = kh * 2 + kci
                    for j in range(8):
                        bank = "P%d" % (j // 2)
                        half = (j % 2) * 256
                        lhsT = sb["sb_w2"][:, spar * 2048 + kci * 1024 + j * 128:
                                           spar * 2048 + kci * 1024 + (j + 1) * 128]
                        rhs = apv(sb["scrb1"], par * 1024 + kc * 256, [[1, 256]])
                        mm(ps[bank][:, half:half + 256], lhsT, rhs, bank,
                           e == 0 and kc == 0 and (j % 2 == 0), False,
                           ["sbw2:p%d" % spar, "scrb1:p%d" % par],
                           tag="w2_%d.%d.%d.%d" % (i, e, kc, j))

        def ffn_tail(i):
            for j in range(8):
                bank = "P%d" % (j // 2)
                half = (j % 2) * 256
                mm(ps[bank][:, half:half + 256], sb["b2_b"][0:8, j * 128:(j + 1) * 128],
                   sb["gcm"][0:8, :], bank, False, False, ["b2_b", "gcm"],
                   tag="b2_%d.%d" % (i, j))
            for kc in range(4):
                for j in range(8):
                    bank = "P%d" % (j // 2)
                    half = (j % 2) * 256
                    lhsT = sb["ws2_b"][:, kc * 1024 + j * 128: kc * 1024 + (j + 1) * 128]
                    rhs = sb["hsh"][:, kc * 256:(kc + 1) * 256]
                    mm(ps[bank][:, half:half + 256], lhsT, rhs, bank,
                       False, kc == 3, ["ws2_b", "hsh"], tag="ws2_%d.%d.%d" % (i, kc, j))
            for j in range(8):
                bank = "P%d" % (j // 2)
                half = (j % 2) * 256
                def fe(eng, j=j, bank=bank, half=half):
                    return eng.activation(sb["g_f"][:, j * 256:(j + 1) * 256],
                                          ps[bank][:, half:half + 256], AF.Silu,
                                          bias=sb["bs2c"][:, j:j + 1])
                S.add("act", fe, reads=(bank, "bs2c"), writes=("g",), tag="ge%d.%d" % (i, j))

        def dbg_dump(name, tensor_key, sbname, cast=True):
            def f(eng, name=name, sbname=sbname):
                return eng.dma_start(out=ap[name], in_=sb[sbname][:, 0:ap[name].shape[1]])
            S.add("gp", f, reads=(tensor_key,), writes=(), dma="dbc", tag="dbg:" + name)

        GSPLIT = 6
        ln_stats(0)
        late_setup()
        for i in range(NCH + 1):
            if i < NCH:
                ln_xn(i)
            if i >= 1:
                scan_group(i - 1, GSPLIT)
            if i < NCH:
                in_proj_phase(i, 0)
            if i >= 1:
                scan_group(i - 1, GSPLIT + 1)
                out_proj_phase_a(i - 1)
                out_proj_a_evac(i - 1, (6, 7, 4, 5, 0, 1, 2, 3))
            if i < NCH:
                conv_phase(i, 0)
            if i >= 1:
                idents_phase(i - 1, GSPLIT, first=True)
            if i < NCH:
                gate_phase(i)
                xproj_phase(i)
                bbcb_phase(i)
            if i >= 1:
                idents_phase(i - 1, GSPLIT + 1, first=True)
            if i < NCH:
                dt_phase(i)
                gate_bcast(i)
                scan_group(i, 0)
                in_proj_phase(i, 1)
                scan_group(i, 1)
                conv_phase(i, 1)
                if i == 0:
                    late_setup_b()
                if DBG_CHUNK == i:
                    dbg_dump("dbg_xz", "xzh", "xzh")
                    dbg_dump("dbg_bb", "Bb", "Bb")
                    dbg_dump("dbg_cb", "Cb", "Cb")
                    dbg_dump("dbg_delta", "delta", "delta")
                    dbg_dump("dbg_u", "u", "u_b")
            if i < NCH:
                for g in range(GD):
                    ffn_expert(i, g)
                    if g < GSPLIT:
                        idents_phase(i, g)
                    if g >= 1:
                        ffn_w2(i, g - 1)
                    hgate_phase(i, g)
                    if g + 2 < GSPLIT:
                        scan_group(i, g + 2)
                ffn_w2(i, GD - 1)
                ws1_phase(i)
            if i >= 1:
                out_proj_phase_b(i - 1)
                combine_phase(i - 1)
                store_phase(i - 1)
            if i < NCH:
                ffn_tail(i)
                xz_tail_save(i, 0)
                xz_tail_save(i, 1)
                if i + 1 < NCH:
                    prefetch(i + 1)
                    ln_stats(i + 1)
                if DBG_CHUNK == i:
                    dbg_dump("dbg_gf", "g", "g_f")
                    dbg_dump("dbg_hg", "scrb1:p1", "scrb1")
                    dbg_dump("dbg_gel", "scrb2:p1", "scrb2")
                    dbg_dump("dbg_ge", "ge_all", "ge_all")

        S.resolve()

        eng_of = {"sp": "sync", "pe": "tensor", "act": "scalar",
                  "dve": "vector", "gp": "gpsimd"}

        with nc.Block() as block:
            def section(engine_name):
                def run(eng):
                    for op in S.ops:
                        if op["e"] != engine_name:
                            continue
                        for (s, v) in op["waits"]:
                            eng.wait_ge(sems[s], v)
                        inst = op["fn"](eng)
                        if getattr(S, "record", None) is not None:
                            try:
                                S.record.append((op["tag"], inst.ins.name))
                            except Exception:
                                pass
                        if op["inc"] is not None:
                            s, step = op["inc"]
                            inst.then_inc(sems[s], step)
                    if engine_name == "sp":
                        eng.wait_ge(sems["dst"], S.final.get("dst", 0))
                return run

            block.sync(section("sp"))
            block.tensor(section("pe"))
            block.scalar(section("act"))
            block.vector(section("dve"))
            block.gpsimd(section("gp"))

    return nc


def host_prep(inputs, L):
    f = np.float32
    W1 = np.asarray(inputs["L0_ffn_W1"], f)
    W2 = np.asarray(inputs["L0_ffn_W2"], f)
    d = {}
    d["w_in"] = np.ascontiguousarray(np.asarray(inputs["L0_attn_in_proj_w"], f).T).reshape(8, 128, 2048)
    d["w_out"] = np.ascontiguousarray(np.asarray(inputs["L0_attn_out_proj_w"], f).T).reshape(16, 128, 1024)
    d["w_w1"] = np.ascontiguousarray(W1.reshape(8, 8, 128, 512))
    d["w_w2"] = np.ascontiguousarray(W2.reshape(8, 4, 128, 1024))
    d["w_ws1"] = np.ascontiguousarray(np.asarray(inputs["L0_ffn_Ws1"], f).reshape(8, 128, 512))
    d["w_ws2"] = np.ascontiguousarray(np.asarray(inputs["L0_ffn_Ws2"], f).reshape(4, 128, 1024))
    cd = np.zeros((2, 4, 128, 1024), f)
    for br, wname in enumerate(["L0_attn_conv_x_w", "L0_attn_conv_z_w"]):
        wk = np.asarray(inputs[wname], f)[:, 0, :]
        for k in range(4):
            for g in range(8):
                idx = np.arange(128)
                cd[br, k, idx, g * 128 + idx] = wk[g * 128 + idx, k]
    d["w_cd"] = cd
    d["aw_alog"] = np.ascontiguousarray(
        np.asarray(inputs["L0_attn_A_log"], f).reshape(8, 128, 16).transpose(1, 0, 2).reshape(128, 128))
    d["aw_dtb"] = np.ascontiguousarray(np.asarray(inputs["L0_attn_dt_proj_b"], f).reshape(8, 128).T)
    dd = np.zeros((128, 1024), f)
    Dv = np.asarray(inputs["L0_attn_D"], f)
    for g in range(8):
        idx = np.arange(128)
        dd[idx, g * 128 + idx] = Dv[g * 128 + idx]
    d["aw_dd"] = dd
    b1 = np.asarray(inputs["L0_ffn_b1"], f)
    d["aw_b1"] = np.ascontiguousarray(b1.reshape(8, 4, 128).transpose(2, 0, 1).reshape(128, 32))
    d["aw_bs1"] = np.ascontiguousarray(np.asarray(inputs["L0_ffn_bs1"], f).reshape(4, 128).T)
    d["aw_bs2"] = np.ascontiguousarray(np.asarray(inputs["L0_ffn_bs2"], f).reshape(8, 128).T)
    d["aw_rs"] = np.full((128, 1), np.asarray(inputs["L0_ffn_route_scale"], f))
    d["aw_xpw"] = np.ascontiguousarray(
        np.asarray(inputs["L0_attn_x_proj_w"], f).T.reshape(8, 128, 96).transpose(1, 0, 2).reshape(128, 768))
    d["aw_dtw"] = np.ascontiguousarray(np.asarray(inputs["L0_attn_dt_proj_w"], f).T)
    d["aw_gw"] = np.ascontiguousarray(
        np.asarray(inputs["L0_ffn_gate_W"], f).reshape(8, 128, 8).transpose(1, 0, 2).reshape(128, 64))
    d["aw_b2"] = np.ascontiguousarray(np.asarray(inputs["L0_ffn_b2"], f))
    d["cident"] = np.eye(128, dtype=f)
    d["cidentb"] = np.eye(128, dtype=ml_dtypes.bfloat16)
    d["cones1"] = np.ones((1, 128), f)
    d["conesc"] = np.ones((128, 1), ml_dtypes.bfloat16)
    return d


def kernel(**inputs):
    x = np.asarray(inputs["L0_attn_hidden_states"], np.float32)
    B, L, Dm = x.shape
    assert Dm == D
    shared = host_prep(inputs, L)
    nc = build_nc(L)
    in_maps = []
    for b in range(B):
        m = dict(shared)
        m["x_t"] = np.ascontiguousarray(x[b].T)
        in_maps.append(m)
    res = run_bass_kernel_spmd(nc, in_maps, core_ids=list(range(B)))
    out = np.stack([np.asarray(res.results[b]["out"]) for b in range(B)], axis=0)
    return out.astype(np.float32)
